# version 07
# baseline (speedup 1.0000x reference)
import math
import os
import numpy as np
import ml_dtypes
import concourse.bass as bass
import concourse.mybir as mybir
from concourse.ap import AP
from concourse.bass_utils import run_bass_kernel_spmd

F32 = mybir.dt.float32
BF16 = mybir.dt.bfloat16
AF = mybir.ActivationFunctionType
ALU = mybir.AluOpType
AX = mybir.AxisListType

HID = 2048
SEQ = 4096
OWN = 2048
NT = 16
NXT = 32
NOT_ = 16
NPAIR = 16
NKT = 32
QCW = 1024
NQC = 2
FF = 1408
NFT = 11
FF2 = 2816
NE = 8
NB = 32
GTAB = 8191

ENGS = ("sync", "gpsimd", "tensor", "vector", "scalar")


def _bucket_np(d):
    rp = np.maximum(d, 0).astype(np.int64)
    is_small = rp < 16
    rpf = np.maximum(rp, 1).astype(np.float32)
    val = np.log(rpf / np.float32(16.0)).astype(np.float32)
    val = (val / np.float32(math.log(128.0 / 16.0)) * np.float32(16.0)).astype(np.float32)
    large = np.minimum(16 + val.astype(np.int32), 31)
    return np.where(is_small, rp, large).astype(np.int64)


def _classify(kt, qc):
    q0 = qc * QCW
    k0 = kt * 128
    if kt < 16:
        dmin, dmax = q0 - (k0 + 127), q0 + QCW - 1 - k0
        if dmax <= 0:
            return ("c0",)
        if dmin >= 113:
            return ("c31",)
        return ("band", 0, q0 - k0 + 4095)
    if q0 - (k0 + 127) + 4096 >= 113:
        return ("pc",)
    return ("band", 32, q0 - k0 + 4095)


_BANDED = [(kt, qc) for qc in range(NQC) for kt in range(NKT)
           if _classify(kt, qc)[0] == "band"]
GW = max(_classify(kt, qc)[2] + QCW for kt, qc in _BANDED) + 1


class Rec:
    def __init__(self):
        self.q = {e: [] for e in ENGS}
        self.meta = {e: [] for e in ENGS}
        self.cnt = {}
        self.cuts = []

    def add(self, e, fn, meta=None):
        self.q[e].append(fn)
        self.meta[e].append(meta)

    def inc(self, name, k=1):
        self.cnt[name] = self.cnt.get(name, 0) + k
        return self.cnt[name]

    def val(self, name):
        return self.cnt.get(name, 0)


SEMNAMES = ["s0", "sgsh", "smm", "sv", "ss", "sph_t", "sph_v", "sph_s",
            "sx0", "sx1", "sw", "swo",
            "sw1a0", "sw1a1", "sw1b0", "sw1b1", "sw2_0", "sw2_1",
            "sbb0", "sbb1",
            "sao", "sat0", "sat1", "srh", "srb", "sac0", "sac1",
            "sgx", "sga", "sout"]


def build(debug=False, cut=None):
    nc = bass.Bass("TRN2", target_bir_lowering=False, debug=False)

    x_d = nc.dram_tensor("x", [SEQ, HID], F32, kind="ExternalInput").ap()
    wq_d = nc.dram_tensor("wq", [HID, HID], F32, kind="ExternalInput").ap()
    wk_d = nc.dram_tensor("wk", [HID, HID], F32, kind="ExternalInput").ap()
    wv_d = nc.dram_tensor("wv", [HID, HID], F32, kind="ExternalInput").ap()
    wo_d = nc.dram_tensor("wo", [HID, HID], F32, kind="ExternalInput").ap()
    wg_d = nc.dram_tensor("wg", [NE, HID], F32, kind="ExternalInput").ap()
    w1_d = nc.dram_tensor("w1", [NE, HID, FF2], F32, kind="ExternalInput").ap()
    w2_d = nc.dram_tensor("w2", [NE, FF, HID], F32, kind="ExternalInput").ap()
    relb_d = nc.dram_tensor("relb", [NB, NB], F32, kind="ExternalInput").ap()
    oho_d = nc.dram_tensor("oho", [NB, GTAB], BF16, kind="ExternalInput").ap()
    ohp_d = nc.dram_tensor("ohp", [NB, GTAB], BF16, kind="ExternalInput").ap()
    idf_d = nc.dram_tensor("identf", [128, 128], F32, kind="ExternalInput").ap()
    out_d = nc.dram_tensor("out", [OWN, HID], F32, kind="ExternalOutput").ap()
    sk = "ExternalOutput" if debug else "Internal"
    g2_d = nc.dram_tensor("g2", [64, GTAB], BF16, kind=sk).ap()
    gsh_d = nc.dram_tensor("gsh", [64 * 128 * GW], BF16, kind="Internal").ap()
    aoT_d = nc.dram_tensor("aoT", [HID, OWN], BF16, kind=sk).ap()
    at32_d = nc.dram_tensor("at32", [OWN, HID], F32, kind=sk).ap()
    rch_d = nc.dram_tensor("rch", [2, QCW], F32, kind="Internal").ap()
    if debug:
        csbg_d = nc.dram_tensor("csbg", [128, NOT_ * 8], F32, kind=sk).ap()

    r = Rec()
    sems = {}

    import contextlib
    with contextlib.ExitStack() as st_:
        for n in SEMNAMES:
            sems[n] = st_.enter_context(nc.semaphore(n))

        xnT = st_.enter_context(nc.sbuf_tensor("xnT", [128, NT * OWN], BF16))
        bigb = st_.enter_context(nc.sbuf_tensor("bigb", [128, 32768], BF16))
        arena = st_.enter_context(nc.sbuf_tensor("arena", [128, 32768], BF16))
        C_sb = st_.enter_context(nc.sbuf_tensor("Csb", [128, NOT_ * 8], F32))
        wgT_hi = st_.enter_context(nc.sbuf_tensor("wgthi", [128, NT * 8], BF16))
        wgT_lo = st_.enter_context(nc.sbuf_tensor("wgtlo", [128, NT * 8], BF16))
        ctab = st_.enter_context(nc.sbuf_tensor("ctab", [128, 96], F32))
        m60 = st_.enter_context(nc.sbuf_tensor("m60", [128, 1], F32))
        idf = st_.enter_context(nc.sbuf_tensor("idf", [128, 128], F32))
        ones_f = st_.enter_context(nc.sbuf_tensor("onesf", [1, 128], F32))
        stats = st_.enter_context(nc.sbuf_tensor("stats", [128, 32], F32))
        dm_sb = st_.enter_context(nc.sbuf_tensor("dm", [128, NOT_], F32))
        eq_sb = st_.enter_context(nc.sbuf_tensor("eq", [128, 2 * NOT_ * 8], BF16))
        w12_sb = st_.enter_context(nc.sbuf_tensor("w12", [128, 2 * NOT_], F32))
        tmp8 = st_.enter_context(nc.sbuf_tensor("tmp8", [128, 8], F32))
        ext3 = st_.enter_context(nc.sbuf_tensor("ext3", [32, 4], F32))
        rowf = st_.enter_context(nc.sbuf_tensor("rowf", [1, 32], F32))
        rc0 = st_.enter_context(nc.sbuf_tensor("rc0", [1, QCW], F32))
        PS = st_.enter_context(nc.psum_tensor("PS", [128, 4096], F32))
        blk = st_.enter_context(nc.Block())

        AR = 32768
        ar_h = arena[:, :].tensor

        def bfv(lo, n):
            return arena[:, lo:lo + n]

        def f32v(lo, n):
            return arena[:, lo:lo + 2 * n].bitcast(F32)

        oh_sb = arena[0:32, 0:GTAB]
        ohp_sb = arena[0:32, 8192:8192 + GTAB]
        gown_sb = arena[32:64, 0:GTAB]
        gpar_sb = arena[32:64, 8192:8192 + GTAB]
        wg_sb = arena[0:8, 16384:20480].bitcast(F32)
        relbf_sb = arena[0:32, 20480:20512]
        relbf32 = arena[0:32, 20512:20576].bitcast(F32)
        xstage = [f32v(0, 2048), f32v(4096, 2048)]
        xnf = [f32v(8192, 2048), f32v(12288, 2048)]
        xnloT = bfv(16384, 2048)
        wq_sb = bfv(0, 2048)
        wk_sb = bfv(2048, 2048)
        wv_sb = bfv(4096, 2048)
        QT2 = bfv(6144, 2048)
        KTz = [bfv(8192, 4096), bfv(12288, 4096)]
        V2 = bfv(16384, NKT * 130)
        abuf = [[bfv(20544 + (h * 2 + p) * QCW, QCW) for p in range(2)] for h in range(2)]
        btile = [bfv(24640 + p * QCW, QCW) for p in range(2)]
        aostage = bfv(26688, QCW)
        rb_sb = f32v(27712, QCW)
        vtf = [f32v(29760 + p * QCW, 512) for p in range(2)]
        aoc = [bfv(0, 2048), bfv(2048, 2048)]
        ostg = [f32v(4096, 2048), f32v(8192, 2048)]
        w1s = [bfv(k * 2048, 2048) for k in range(4)]
        w2c = [bfv(8192 + k * NFT * 512, NFT * 512) for k in range(2)]
        gT = bfv(8192 + 2 * NFT * 512, NFT * QCW)
        sl_sb = f32v(30720, 1024)
        outacc = bigb[:, :].bitcast(F32)

        def W(e, sem, val):
            if val:
                r.add(e, lambda E, h=sems[sem], v=val: E.wait_ge(h, v),
                      ("wait", sem, val))

        def WM(e, mark):
            if mark is not None:
                W(e, mark[0], mark[1])

        def DMA(e, sem, out, in_, **kw):
            v = r.inc(sem, 16)

            def f(E, o=out, i=in_, h=sems[sem], k=dict(kw)):
                E.dma_start(o, i, **k).then_inc(h, 16)
            r.add(e, f, ("inc", sem, 16))
            return v

        def OP(e, sem, f, inc=1):
            if inc:
                v = r.inc(sem, inc)
                r.add(e, lambda E, ff=f, h=sems[sem], k=inc: ff(E).then_inc(h, k),
                      ("inc", sem, inc))
                return v
            r.add(e, lambda E, ff=f: ff(E))
            return r.val(sem)

        def VOP(f):
            return OP("vector", "sv", f)

        def VCH():
            W("vector", "sv", r.val("sv"))

        def MM(out, lhsT, rhs, start, stop, inc=0):
            if inc:
                v = r.inc("smm")
                r.add("tensor", lambda E, o=out, l=lhsT, rr=rhs, s=start, p=stop,
                      h=sems["smm"]: E.matmul(o, l, rr, start=s, stop=p).then_inc(h, 1),
                      ("inc", "smm", 1))
                return v
            r.add("tensor", lambda E, o=out, l=lhsT, rr=rhs, s=start, p=stop:
                  E.matmul(o, l, rr, start=s, stop=p))
            return None

        def barrier():
            r.cuts.append({e: len(r.q[e]) for e in ENGS})
            mk = {"tensor": OP("tensor", "sph_t", lambda E: E.nop()),
                  "vector": OP("vector", "sph_v", lambda E: E.nop()),
                  "scalar": OP("scalar", "sph_s", lambda E: E.nop())}
            pairs = {"tensor": "sph_t", "vector": "sph_v", "scalar": "sph_s"}
            for e in pairs:
                for e2, sem in pairs.items():
                    if e2 != e:
                        W(e, sem, mk[e2])

        DMA("sync", "s0", idf[:, :], idf_d)
        DMA("sync", "s0", relbf32[:, :], relb_d)
        DMA("sync", "s0", wg_sb[:, :], wg_d)
        DMA("sync", "s0", oh_sb[:, :], oho_d)
        v0 = DMA("sync", "s0", ohp_sb[:, :], ohp_d)
        W("vector", "s0", v0)
        VOP(lambda E: E.tensor_copy(relbf_sb[:, :], relbf32[:, :]))
        VOP(lambda E: E.memset(ones_f[:, :], 1.0))
        v_tab = VOP(lambda E: E.memset(m60[:, :], -60.0))

        W("tensor", "sv", v_tab)
        CH = [(c * 512, min(512, GTAB - c * 512)) for c in range(16)]
        extract = {(0, 7): (4095 - 7 * 512, 0), (0, 8): (4208 - 8 * 512, 1),
                   (1, 3): (2000 - 3 * 512, 2)}
        bank_free = [None, None]
        for ri, (ohs, gdst) in enumerate(((oh_sb, gown_sb), (ohp_sb, gpar_sb))):
            for ci, (c0, cw) in enumerate(CH):
                bk = (ri * 16 + ci) % 2
                WM("tensor", bank_free[bk])
                mv = MM(PS[0:32, bk * 512:bk * 512 + cw], relbf_sb[:, :],
                        ohs[:, c0:c0 + cw], True, True, inc=1)
                W("vector", "smm", mv)
                VOP(lambda E, g=gdst, a=c0, wd=cw, b=bk * 512:
                    E.tensor_copy(g[:, a:a + wd], PS[0:32, b:b + wd]))
                if (ri, ci) in extract:
                    off, kx = extract[(ri, ci)]
                    VCH()
                    VOP(lambda E, o=off, kk=kx, b=bk * 512:
                        E.tensor_copy(ext3[:, kk:kk + 1], PS[0:32, b + o:b + o + 1]))
                bank_free[bk] = ("sv", r.val("sv"))
        v_g = r.val("sv")
        W("sync", "sv", v_g)
        DMA("sync", "s0", g2_d[0:32, :], gown_sb[:, :])
        vg2 = DMA("sync", "s0", g2_d[32:64, :], gpar_sb[:, :])
        W("sync", "s0", vg2)
        W("scalar", "s0", r.val("s0"))
        for p in range(128):
            ln = GW - p
            DMA("scalar", "sgsh",
                AP(gsh_d.tensor, p * GW + p, [[128 * GW, 64], [1, ln]]),
                AP(g2_d.tensor, 0, [[GTAB, 64], [1, ln]]))
        gsh_full = r.val("sgsh")
        for kx in range(3):
            tv = OP("tensor", "smm", lambda E, kk=kx:
                    E.transpose(PS[0:1, 1536:1568], ext3[:, kk:kk + 1], idf[0:32, 0:32]))
            W("vector", "smm", tv)
            cv = VOP(lambda E: E.tensor_copy(rowf[:, :], PS[0:1, 1536:1568]))
            W("tensor", "sv", cv)
            bv = OP("tensor", "smm", lambda E:
                    E.matmul(PS[:, 1024:1056], ones_f[:, :], rowf[:, :],
                             start=True, stop=True))
            W("vector", "smm", bv)
            VOP(lambda E, kk=kx: E.tensor_scalar(
                ctab[:, kk * 32:(kk + 1) * 32], PS[:, 1024:1056], -60.0, None, ALU.add))
        for t in range(NT):
            tv = OP("tensor", "smm", lambda E, tt=t:
                    E.transpose(PS[:, 1536 + (tt % 2) * 16:1544 + (tt % 2) * 16],
                                wg_sb[:, tt * 128:(tt + 1) * 128], idf[0:8, 0:8]))
            W("vector", "smm", tv)
            VOP(lambda E, tt=t: E.tensor_copy(
                wgT_hi[:, tt * 8:(tt + 1) * 8],
                PS[:, 1536 + (tt % 2) * 16:1544 + (tt % 2) * 16]))
            VCH()
            vv = VOP(lambda E, tt=t: E.tensor_tensor(
                wgT_lo[:, tt * 8:(tt + 1) * 8],
                PS[:, 1536 + (tt % 2) * 16:1544 + (tt % 2) * 16],
                wgT_hi[:, tt * 8:(tt + 1) * 8], ALU.subtract))
            W("tensor", "sv", vv)

        barrier()

        S_s1, S_mean, S_s2, S_var, S_msq, S_std, S_rstd = range(7)
        S_m1, S_m2 = 8, 9

        def stv(par, k):
            return stats[:, par * 16 + k:par * 16 + k + 1]

        consumed = {}
        slot_ev = [None] * 4
        gate_ps_free = None
        prev_gmm = None
        xn_done = {}
        tr_mark = {}

        def emit_ln_a(xt):
            par = xt % 2
            sxn = "sx0" if par == 0 else "sx1"
            if xt >= 2:
                sm, vm = consumed[xt - 2]
                W("sync", "ss", sm)
                W("sync", "sv", vm)
            vx = DMA("sync", sxn, xstage[par][:, :], x_d[xt * 128:(xt + 1) * 128, :])
            W("vector", sxn, vx)
            VOP(lambda E, p=par: E.reduce_sum(stv(p, S_s1), xstage[p][:, :], axis=AX.X))
            if xt - 2 in tr_mark:
                W("vector", "smm", tr_mark[xt - 2])
            VOP(lambda E, p=par: E.tensor_tensor(xnf[p][:, :], xstage[p][:, :],
                                                 xstage[p][:, :], ALU.mult))
            VCH()
            VOP(lambda E, p=par: E.reduce_sum(stv(p, S_s2), xnf[p][:, :], axis=AX.X))
            VOP(lambda E, p=par: E.tensor_scalar(stv(p, S_mean), stv(p, S_s1),
                                                 1.0 / HID, None, ALU.mult))
            VCH()
            VOP(lambda E, p=par: E.tensor_scalar(stv(p, S_var), stv(p, S_s2),
                                                 1.0 / HID, 1e-5, ALU.mult, ALU.add))
            VOP(lambda E, p=par: E.tensor_tensor(stv(p, S_msq), stv(p, S_mean),
                                                 stv(p, S_mean), ALU.mult))
            VCH()
            vv = VOP(lambda E, p=par: E.tensor_tensor(stv(p, S_var), stv(p, S_var),
                                                      stv(p, S_msq), ALU.subtract))
            W("scalar", "sv", vv)
            OP("scalar", "ss", lambda E, p=par: E.activation(
                stv(p, S_std), stv(p, S_var), AF.Ln))
            W("scalar", "ss", r.val("ss"))
            sv_ = OP("scalar", "ss", lambda E, p=par: E.activation(
                stv(p, S_rstd), stv(p, S_std), AF.Exp, scale=-0.5))
            consumed[xt] = (sv_, None)
            return sv_

        def emit_ln_b(xt, sv_):
            par = xt % 2
            W("vector", "ss", sv_)
            vv = VOP(lambda E, p=par: E.tensor_scalar(
                xnf[p][:, :], xstage[p][:, :], stv(p, S_mean), stv(p, S_rstd),
                ALU.subtract, ALU.mult))
            consumed[xt] = (sv_, vv)
            xn_done[xt] = vv

        def emit_tr(xt):
            nonlocal gate_ps_free, prev_gmm
            par = xt % 2
            own = xt < NOT_
            W("tensor", "sv", xn_done[xt])
            if own and prev_gmm is not None:
                W("vector", "smm", prev_gmm)
            for t in range(NT):
                slot = t % 4
                WM("tensor", slot_ev[slot])
                tv = OP("tensor", "smm", lambda E, tt=t, s=slot, p=par: E.transpose(
                    PS[:, s * 512:s * 512 + 128], xnf[p][:, tt * 128:(tt + 1) * 128],
                    idf[:, :]))
                W("vector", "smm", tv)
                dst = (xnT[:, t * OWN + xt * 128: t * OWN + (xt + 1) * 128] if own
                       else bigb[:, t * OWN + (xt - NOT_) * 128: t * OWN + (xt - NOT_ + 1) * 128])
                VOP(lambda E, d=dst, s=slot: E.tensor_copy(d, PS[:, s * 512:s * 512 + 128]))
                if own:
                    VCH()
                    VOP(lambda E, d=dst, s=slot, tt=t: E.tensor_tensor(
                        xnloT[:, tt * 128:(tt + 1) * 128],
                        PS[:, s * 512:s * 512 + 128], d, ALU.subtract))
                slot_ev[slot] = ("sv", r.val("sv"))
            if own:
                vlo = r.val("sv")
                W("tensor", "sv", vlo)
                WM("tensor", gate_ps_free)
                n_mm = 0
                for t in range(NT):
                    hi = xnT[:, t * OWN + xt * 128: t * OWN + (xt + 1) * 128]
                    lo = xnloT[:, t * 128:(t + 1) * 128]
                    whi = wgT_hi[:, t * 8:(t + 1) * 8]
                    wlo = wgT_lo[:, t * 8:(t + 1) * 8]
                    for lh, rh in ((hi, whi), (hi, wlo), (lo, whi), (lo, wlo)):
                        n_mm += 1
                        MM(PS[:, 2048:2056], lh, rh, n_mm == 1, n_mm == 64,
                           inc=1 if n_mm == 64 else 0)
                prev_gmm = r.val("smm")
                W("vector", "smm", prev_gmm)
                VOP(lambda E: E.reduce_max(stv(0, S_m1), PS[:, 2048:2056], axis=AX.X))
                VCH()
                VOP(lambda E, q=xt: E.tensor_scalar(
                    eq_sb[:, q * 8:(q + 1) * 8], PS[:, 2048:2056], stv(0, S_m1),
                    None, ALU.is_equal))
                VCH()
                VOP(lambda E, q=xt: E.scalar_tensor_tensor(
                    tmp8[:, :], eq_sb[:, q * 8:(q + 1) * 8], -1e30, PS[:, 2048:2056],
                    ALU.mult, ALU.add))
                VCH()
                VOP(lambda E: E.reduce_max(stv(0, S_m2), tmp8[:, :], axis=AX.X))
                VCH()
                VOP(lambda E, q=xt: E.tensor_tensor(
                    dm_sb[:, q:q + 1], stv(0, S_m1), stv(0, S_m2), ALU.subtract))
                vv = VOP(lambda E, q=xt: E.tensor_scalar(
                    eq_sb[:, (NOT_ + q) * 8:(NOT_ + q + 1) * 8], tmp8[:, :],
                    stv(0, S_m2), None, ALU.is_equal))
                gate_ps_free = ("sv", vv)
            tr_mark[xt] = r.val("smm")

        ln_marks = {}
        for xt in range(NXT):
            ln_marks[xt] = emit_ln_a(xt)
            if xt > 0:
                emit_tr(xt - 1)
            emit_ln_b(xt, ln_marks[xt])
        emit_tr(NXT - 1)
        W("scalar", "sv", r.val("sv"))
        sv_ = OP("scalar", "ss", lambda E: E.activation(
            w12_sb[:, 0:NOT_], dm_sb[:, :], AF.Sigmoid))
        W("vector", "ss", sv_)
        VOP(lambda E: E.tensor_scalar(w12_sb[:, NOT_:2 * NOT_], w12_sb[:, 0:NOT_],
                                      -1.0, 1.0, ALU.mult, ALU.add))
        VCH()
        for q in range(NOT_):
            VOP(lambda E, qq=q: E.tensor_scalar(
                tmp8[:, :], eq_sb[:, qq * 8:(qq + 1) * 8],
                w12_sb[:, qq:qq + 1], None, ALU.mult))
            VCH()
            VOP(lambda E, qq=q: E.scalar_tensor_tensor(
                C_sb[:, qq * 8:(qq + 1) * 8],
                eq_sb[:, (NOT_ + qq) * 8:(NOT_ + qq + 1) * 8],
                w12_sb[:, NOT_ + qq:NOT_ + qq + 1], tmp8[:, :], ALU.mult, ALU.add))
            VCH()
        vC = r.val("sv")
        if debug:
            W("sync", "sv", vC)
            DMA("sync", "s0", csbg_d, C_sb[:, :])

        barrier()
        VOP(lambda E: E.memset(KTz[0][64:128, :], 0.0))
        VOP(lambda E: E.memset(KTz[1][0:64, :], 0.0))
        VOP(lambda E: E.memset(AP(ar_h, 16384 + 64, [[AR, 128], [130, NKT]]), 1.0))
        v_b0 = VOP(lambda E: E.memset(AP(ar_h, 16384 + 129, [[AR, 128], [130, NKT]]), 1.0))
        W("tensor", "sv", v_b0)

        W("sync", "sgsh", gsh_full)

        def wslice(wd, p):
            return [AP(wd.tensor, t0 * 128 * HID + p * 128,
                       [[HID, 128], [128 * HID, 4], [1, 128]]) for t0 in (0, 4, 8, 12)]

        def load_pair_w(p, wait_mark):
            for wd, dst in ((wq_d, wq_sb), (wk_d, wk_sb), (wv_d, wv_sb)):
                for i, src in enumerate(wslice(wd, p)):
                    WM("gpsimd", wait_mark)
                    wait_mark = None
                    DMA("gpsimd", "sw", dst[:, i * 512:(i + 1) * 512], src)
            return r.val("sw")

        w_ready = load_pair_w(0, ("sv", vC))
        exp_by_slot = [0, 0]
        av_by_abuf = {}
        psA_free = [None, None]
        psS_free = [None, None]
        vproj_end = {}
        sao_last = 0
        srb_last = 0
        srh_last = 0
        ao_mult_last = None
        bias_slot_free = [None, None]

        for p in range(NPAIR):
            W("tensor", "sw", w_ready)
            W("tensor", "ss", max(exp_by_slot))
            for t in range(NT):
                for c4 in range(4):
                    MM(PS[:, c4 * 512:(c4 + 1) * 512], wq_sb[:, t * 128:(t + 1) * 128],
                       xnT[:, t * OWN + c4 * 512: t * OWN + (c4 + 1) * 512],
                       t == 0, t == NT - 1,
                       inc=1 if (t == NT - 1 and c4 == 3) else 0)
            qmm = r.val("smm")
            W("vector", "smm", qmm)
            VOP(lambda E: E.tensor_copy(QT2[:, :], PS[:, 0:2048]))
            kv_ev = [None, None]
            WM("tensor", psA_free[0])
            WM("tensor", psA_free[1])
            for kc in range(8):
                bk = 2048 + (kc % 2) * 512
                src_blk = xnT if kc < 4 else bigb
                WM("tensor", kv_ev[kc % 2])
                for t in range(NT):
                    MM(PS[:, bk:bk + 512], wk_sb[:, t * 128:(t + 1) * 128],
                       src_blk[:, t * OWN + (kc % 4) * 512: t * OWN + (kc % 4 + 1) * 512],
                       t == 0, t == NT - 1, inc=1 if t == NT - 1 else 0)
                kmm = r.val("smm")
                W("vector", "smm", kmm)
                VOP(lambda E, b=bk, c=kc: E.tensor_copy(
                    KTz[0][0:64, c * 512:(c + 1) * 512], PS[0:64, b:b + 512]))
                VOP(lambda E, b=bk, c=kc: E.tensor_copy(
                    KTz[1][64:128, c * 512:(c + 1) * 512], PS[64:128, b:b + 512]))
                kv_ev[kc % 2] = ("sv", r.val("sv"))
            tr_ev = [None, None]
            for kc in range(8):
                bk = 2048 + (kc % 2) * 512
                src_blk = xnT if kc < 4 else bigb
                WM("tensor", kv_ev[kc % 2])
                for t in range(NT):
                    MM(PS[:, bk:bk + 512], wv_sb[:, t * 128:(t + 1) * 128],
                       src_blk[:, t * OWN + (kc % 4) * 512: t * OWN + (kc % 4 + 1) * 512],
                       t == 0, t == NT - 1, inc=1 if t == NT - 1 else 0)
                vmm = r.val("smm")
                W("vector", "smm", vmm)
                VOP(lambda E, b=bk, c=kc: E.tensor_copy(vtf[c % 2][:, :], PS[:, b:b + 512]))
                kv_ev[kc % 2] = ("sv", r.val("sv"))
                W("tensor", "sv", kv_ev[kc % 2][1])
                for k4 in range(4):
                    kt = kc * 4 + k4
                    to = 3072 + (k4 % 2) * 512
                    WM("tensor", tr_ev[k4 % 2])
                    tv = OP("tensor", "smm", lambda E, c=kc, kk=k4, o=to: E.transpose(
                        PS[:, o:o + 128], vtf[c % 2][:, kk * 128:(kk + 1) * 128],
                        idf[:, :]))
                    W("vector", "smm", tv)
                    VOP(lambda E, c=kt, o=to: E.tensor_copy(
                        V2[:, c * 130:c * 130 + 64], PS[:, o:o + 64]))
                    VOP(lambda E, c=kt, o=to: E.tensor_copy(
                        V2[:, c * 130 + 65:c * 130 + 129], PS[:, o + 64:o + 128]))
                    tr_ev[k4 % 2] = ("sv", r.val("sv"))
            vproj_end[p] = r.val("smm")
            if p + 1 < NPAIR:
                w_ready = load_pair_w(p + 1, ("smm", vproj_end[p]))
            else:
                for t in range(NT):
                    WM("gpsimd", ("smm", vproj_end[p]) if t == 0 else None)
                    DMA("gpsimd", "swo", bigb[:, t * 2048:(t + 1) * 2048],
                        AP(wo_d.tensor, t * 128 * HID, [[HID, 128], [1, HID]]))
            W("tensor", "sv", r.val("sv"))

            for h in range(2):
                hg = 2 * p + h
                for qc in range(NQC):
                    aslot = qc % 2
                    psA0 = 2048 + aslot * 1024

                    def emit_S(kt):
                        cls = _classify(kt, qc)
                        sslot = kt % 2
                        psS = PS[:, sslot * 1024:(sslot + 1) * 1024]
                        W("tensor", "ss", exp_by_slot[sslot])
                        for j in range(2):
                            MM(PS[:, sslot * 1024 + j * 512:sslot * 1024 + (j + 1) * 512],
                               KTz[h][:, kt * 128:(kt + 1) * 128],
                               QT2[:, qc * 1024 + j * 512:qc * 1024 + (j + 1) * 512],
                               True, True, inc=1 if j == 1 else 0)
                        smm_v = r.val("smm")
                        bias_m = None
                        if cls[0] == "band":
                            row = cls[1] + hg
                            i0 = cls[2]
                            sbn = "sbb%d" % sslot
                            WM("sync", bias_slot_free[sslot])
                            bias_slot_free[sslot] = None
                            vb = DMA("sync", sbn, btile[sslot][:, :],
                                     AP(gsh_d.tensor, row * 128 * GW + i0,
                                        [[GW, 128], [1, QCW]]))
                            W("vector", sbn, vb)
                            W("vector", "smm", smm_v)
                            bias_m = VOP(lambda E, ps=psS, bt=btile[sslot]:
                                         E.tensor_tensor(ps, ps, bt, ALU.add))
                            bcol = m60[:, 0:1]
                        elif cls[0] == "c0":
                            bcol = ctab[:, hg:hg + 1]
                        elif cls[0] == "c31":
                            bcol = ctab[:, 32 + hg:32 + hg + 1]
                        else:
                            bcol = ctab[:, 64 + hg:64 + hg + 1]
                        if bias_m is not None:
                            W("scalar", "sv", bias_m)
                        else:
                            W("scalar", "smm", smm_v)
                        am = av_by_abuf.get(kt % 2)
                        if am is not None:
                            W("scalar", "smm", am)
                        ev = OP("scalar", "ss", lambda E, ps=psS,
                                ab=abuf[0][kt % 2], bc=bcol:
                                E.activation(ab, ps, AF.Exp, bias=bc))
                        exp_by_slot[sslot] = ev
                        if bias_m is not None:
                            bias_slot_free[sslot] = ("ss", ev)

                    def emit_AV(kt):
                        W("tensor", "ss", exp_by_slot[kt % 2])
                        if kt == 0:
                            WM("tensor", psA_free[aslot])
                            psA_free[aslot] = None
                        for j in range(2):
                            MM(PS[0:65, psA0 + j * 512:psA0 + (j + 1) * 512],
                               V2[:, kt * 130 + h * 65:kt * 130 + h * 65 + 65],
                               abuf[0][kt % 2][:, j * 512:(j + 1) * 512],
                               kt == 0, kt == NKT - 1, inc=1 if j == 1 else 0)
                        av_by_abuf[kt % 2] = r.val("smm")

                    for kt in range(NKT):
                        emit_S(kt)
                        if kt > 0:
                            emit_AV(kt - 1)
                    emit_AV(NKT - 1)
                    avl = av_by_abuf[(NKT - 1) % 2]
                    W("vector", "smm", avl)
                    WM("vector", ao_mult_last)
                    vr = VOP(lambda E, o=psA0: E.reciprocal(
                        rb_sb[64:65, :], PS[64:65, o:o + 1024]))
                    W("sync", "sv", vr)
                    W("sync", "srb", srb_last)
                    rsl = (2 * hg + qc) % 2
                    srh_last = DMA("sync", "srh", rch_d[rsl:rsl + 1, :], rb_sb[64:65, :])
                    W("sync", "srh", srh_last)
                    srb_last = DMA("sync", "srb", rb_sb[0:64, :],
                                   AP(rch_d.tensor, rsl * QCW, [[0, 64], [1, QCW]]))
                    W("vector", "srb", srb_last)
                    if sao_last:
                        W("vector", "sao", sao_last)
                    vao = VOP(lambda E, o=psA0: E.tensor_tensor(
                        aostage[0:64, :], PS[0:64, o:o + 1024],
                        rb_sb[0:64, :], ALU.mult))
                    ao_mult_last = ("sv", vao)
                    psA_free[aslot] = ("sv", vao)
                    W("sync", "sv", vao)
                    sao_last = DMA("sync", "sao",
                                   aoT_d[hg * 64:(hg + 1) * 64, qc * 1024:(qc + 1) * 1024],
                                   aostage[0:64, :])

        barrier()

        r.cuts.append({e: len(r.q[e]) for e in ENGS})
        if not os.environ.get("K_PC_NOSAO"):
            W("sync", "sao", r.val("sao"))
        if not os.environ.get("K_PC_NOWO"):
            W("tensor", "swo", r.val("swo"))
        o_ev = [None, None]
        o_mm_mark = [0, 0]
        sat_hist = {}
        for tt in range(NOT_):
            par = tt % 2
            sem = "sac0" if par == 0 else "sac1"
            if tt >= 2:
                W("sync", "smm", o_mm_mark[par])
            vl = 0
            if not os.environ.get("K_PC_NOLD"):
                for t0 in (0, 4, 8, 12):
                    vl = DMA("sync", sem, aoc[par][:, t0 * 128:(t0 + 4) * 128],
                             AP(aoT_d.tensor, t0 * 128 * OWN + tt * 128,
                                [[OWN, 128], [128 * OWN, 4], [1, 128]]))
                W("tensor", sem, vl)
            WM("tensor", o_ev[par])
            if os.environ.get("K_PC_NOMM"):
                o_mm_mark[par] = OP("tensor", "smm", lambda E: E.nop())
            else:
                for t in range(NT):
                    for hc in range(4):
                        MM(PS[:, par * 2048 + hc * 512:par * 2048 + (hc + 1) * 512],
                           aoc[par][:, t * 128:(t + 1) * 128],
                           bigb[:, t * 2048 + hc * 512:t * 2048 + (hc + 1) * 512],
                           t == 0, t == NT - 1,
                           inc=1 if (t == NT - 1 and hc == 3) else 0)
                o_mm_mark[par] = r.val("smm")
            W("vector", "smm", o_mm_mark[par])
            satn = "sat0" if par == 0 else "sat1"
            if tt >= 2 and sat_hist.get(tt - 2):
                W("vector", satn, sat_hist[tt - 2])
            if os.environ.get("K_PC_NOST"):
                vv = VOP(lambda E: E.nop())
                o_ev[par] = ("sv", vv)
                sat_hist[tt] = 0
            else:
                vv = VOP(lambda E, pp=par: E.tensor_copy(ostg[pp][:, :],
                                                         PS[:, pp * 2048:(pp + 1) * 2048]))
                o_ev[par] = ("sv", vv)
                W("sync", "sv", vv)
                sat_hist[tt] = DMA("sync", satn, at32_d[tt * 128:(tt + 1) * 128, :],
                                   ostg[par][:, :])

        barrier()

        omm_last = r.val("smm")
        m_gTm = {}
        m_sttbank = [None] * 4
        m_f1 = {}
        stt_prev = {}
        sl_free = None
        for half in range(2):
            W("gpsimd", "sat0", r.val("sat0"))
            W("gpsimd", "sat1", r.val("sat1"))
            W("gpsimd", "smm", omm_last)
            if half == 1:
                W("gpsimd", "sout", 16)
            for tt in range(8):
                DMA("gpsimd", "sgx", outacc[:, tt * 2048:(tt + 1) * 2048],
                    x_d[half * 1024 + tt * 128: half * 1024 + (tt + 1) * 128, :])
            W("gpsimd", "sgx", r.val("sgx"))
            for tt in range(8):
                DMA("gpsimd", "sga", outacc[:, tt * 2048:(tt + 1) * 2048],
                    at32_d[half * 1024 + tt * 128: half * 1024 + (tt + 1) * 128, :],
                    accum_op=ALU.add)
            sga_full = r.val("sga")
            for e in range(NE):
                for j in range(NFT):
                    gi = half * NE * NFT + e * NFT + j
                    slot = gi % 2
                    wm = m_f1.get(gi - 2)
                    for ab, foff, sname in ((0, 0, "sw1a%d" % slot),
                                            (2, FF, "sw1b%d" % slot)):
                        WM("gpsimd", ("smm", wm) if wm else None)
                        wm = None
                        DMA("gpsimd", sname, w1s[ab + slot][:, :],
                            AP(w1_d.tensor, e * HID * FF2 + foff + j * 128,
                               [[FF2, 128], [128 * FF2, NT], [1, 128]]))
                    a_ps = 0 if slot == 0 else 2048
                    b_ps = 1024 if slot == 0 else 3072
                    W("tensor", "sw1a%d" % slot, r.val("sw1a%d" % slot))
                    W("tensor", "sw1b%d" % slot, r.val("sw1b%d" % slot))
                    WM("tensor", m_gTm.get(gi - 2))
                    if slot == 0:
                        for bq in range(4):
                            WM("tensor", m_sttbank[bq])
                            m_sttbank[bq] = None
                    for ab, ps0 in ((0, a_ps), (2, b_ps)):
                        for t in range(NT):
                            for c2 in range(2):
                                MM(PS[:, ps0 + c2 * 512:ps0 + (c2 + 1) * 512],
                                   w1s[ab + slot][:, t * 128:(t + 1) * 128],
                                   xnT[:, t * OWN + half * 1024 + c2 * 512:
                                       t * OWN + half * 1024 + (c2 + 1) * 512],
                                   t == 0, t == NT - 1,
                                   inc=1 if (t == NT - 1 and c2 == 1) else 0)
                    m_f1[gi] = r.val("smm")
                    W("scalar", "smm", m_f1[gi])
                    WM("scalar", sl_free)
                    sv_ = OP("scalar", "ss", lambda E, a=a_ps: E.activation(
                        sl_sb[:, :], PS[:, a:a + 1024], AF.Silu))
                    W("vector", "ss", sv_)
                    vv = VOP(lambda E, b=b_ps, jj=j: E.tensor_tensor(
                        gT[:, jj * QCW:(jj + 1) * QCW], sl_sb[:, :],
                        PS[:, b:b + 1024], ALU.mult))
                    m_gTm[gi] = ("sv", vv)
                    sl_free = ("sv", vv)
                gT_ready = m_gTm[half * NE * NFT + e * NFT + NFT - 1]
                for hc in range(4):
                    g2i = half * NE * 4 + e * 4 + hc
                    slot = g2i % 2
                    wfree = m_f1.get(gi) if g2i < 2 else None
                    w2war = stt_prev.get(("w2", slot))
                    WM("gpsimd", w2war)
                    DMA("gpsimd", "sw2_%d" % slot, w2c[slot][:, :],
                        AP(w2_d.tensor, e * FF * HID + hc * 512,
                           [[HID, 128], [128 * HID, NFT], [1, 512]]))
                    w2v = r.val("sw2_%d" % slot)
                    W("tensor", "sw2_%d" % slot, w2v)
                    WM("tensor", gT_ready)
                    gT_ready = None
                    for tt in range(8):
                        bk = tt % 4
                        WM("tensor", m_sttbank[bk])
                        m_sttbank[bk] = None
                        for ft in range(NFT):
                            MM(PS[:, bk * 512:(bk + 1) * 512],
                               gT[:, ft * QCW + tt * 128:ft * QCW + (tt + 1) * 128],
                               w2c[slot][:, ft * 512:(ft + 1) * 512],
                               ft == 0, ft == NFT - 1,
                               inc=1 if ft == NFT - 1 else 0)
                        ymm = r.val("smm")
                        W("vector", "smm", ymm)
                        W("vector", "sga", sga_full)
                        pk = (half, tt, hc)
                        if pk in stt_prev:
                            W("vector", "sv", stt_prev[pk])
                        vv = VOP(lambda E, t_=tt, h_=hc, b=bk, ee=e, hf=half:
                                 E.scalar_tensor_tensor(
                                     outacc[:, t_ * 2048 + h_ * 512:
                                            t_ * 2048 + (h_ + 1) * 512],
                                     PS[:, b * 512:(b + 1) * 512],
                                     C_sb[:, (hf * 8 + t_) * 8 + ee:
                                          (hf * 8 + t_) * 8 + ee + 1],
                                     outacc[:, t_ * 2048 + h_ * 512:
                                            t_ * 2048 + (h_ + 1) * 512],
                                     ALU.mult, ALU.add))
                        stt_prev[pk] = vv
                        m_sttbank[bk] = ("sv", vv)
                    stt_prev[("w2", slot)] = ("smm", r.val("smm"))
            W("sync", "sv", r.val("sv"))
            DMA("sync", "sout",
                AP(out_d.tensor, half * 1024 * HID,
                   [[HID, 128], [128 * HID, 8], [1, HID]]),
                outacc[:, :].rearrange("p (t c) -> p t c", t=8))
        W("sync", "sout", r.val("sout"))

        for e in ENGS:
            n = len(r.q[e]) if cut is None else r.cuts[cut][e]
            getattr(blk, e)(lambda E, Q=tuple(r.q[e][:n]): [f(E) for f in Q])

    build.last_rec = r
    return nc


_NC_CACHE = {}


def _get_nc(debug=False):
    if debug not in _NC_CACHE:
        _NC_CACHE[debug] = build(debug)
    return _NC_CACHE[debug]


def _onehot_tables(half):
    i = np.arange(GTAB, dtype=np.int64)
    d_own = i - 4095
    d_par = d_own if half == 0 else d_own + 4096
    oh_o = np.zeros((NB, GTAB), dtype=ml_dtypes.bfloat16)
    oh_p = np.zeros((NB, GTAB), dtype=ml_dtypes.bfloat16)
    oh_o[_bucket_np(d_own), i] = 1
    oh_p[_bucket_np(d_par), i] = 1
    return oh_o, oh_p


def _make_inputs(inputs):
    x = np.asarray(inputs["L0_attn_hidden_states"])
    common = {
        "wq": np.ascontiguousarray(inputs["L0_attn_Wq"], dtype=np.float32),
        "wk": np.ascontiguousarray(inputs["L0_attn_Wk"], dtype=np.float32),
        "wv": np.ascontiguousarray(inputs["L0_attn_Wv"], dtype=np.float32),
        "wo": np.ascontiguousarray(inputs["L0_attn_Wo"], dtype=np.float32),
        "relb": np.ascontiguousarray(inputs["L0_attn_rel_bias"], dtype=np.float32),
        "wg": np.ascontiguousarray(inputs["L0_ffn_wg"], dtype=np.float32),
        "w1": np.ascontiguousarray(inputs["L0_ffn_w1"], dtype=np.float32),
        "w2": np.ascontiguousarray(inputs["L0_ffn_w2"], dtype=np.float32),
        "identf": np.eye(128, dtype=np.float32),
    }
    ohs = [_onehot_tables(h) for h in (0, 1)]
    in_maps = []
    for c in range(8):
        b, half = c // 2, c % 2
        xr = np.ascontiguousarray(np.concatenate(
            [x[b, half * OWN:(half + 1) * OWN],
             x[b, (1 - half) * OWN:(2 - half) * OWN]], axis=0).astype(np.float32))
        m = dict(common)
        m["x"] = xr
        m["oho"], m["ohp"] = ohs[half]
        in_maps.append(m)
    return in_maps


def kernel(**inputs):
    nc = _get_nc(False)
    in_maps = _make_inputs(inputs)
    res = run_bass_kernel_spmd(nc, in_maps, core_ids=list(range(8)))
    out = np.empty((4, SEQ, HID), dtype=np.float32)
    for c in range(8):
        b, half = c // 2, c % 2
        out[b, half * OWN:(half + 1) * OWN] = res.results[c]["out"]
    return out


# version 08 (final)
# speedup vs baseline: 1.0736x; 1.0736x over previous
import math
import os
import numpy as np
import ml_dtypes
import concourse.bass as bass
import concourse.mybir as mybir
from concourse.ap import AP
from concourse.bass_utils import run_bass_kernel_spmd

F32 = mybir.dt.float32
BF16 = mybir.dt.bfloat16
AF = mybir.ActivationFunctionType
ALU = mybir.AluOpType
AX = mybir.AxisListType

HID = 2048
SEQ = 4096
OWN = 2048
NT = 16
NXT = 32
NOT_ = 16
NPAIR = 16
NKT = 32
QCW = 1024
NQC = 2
FF = 1408
NFT = 11
FF2 = 2816
NE = 8
NB = 32
GTAB = 8191

ENGS = ("sync", "gpsimd", "tensor", "vector", "scalar")


def _bucket_np(d):
    rp = np.maximum(d, 0).astype(np.int64)
    is_small = rp < 16
    rpf = np.maximum(rp, 1).astype(np.float32)
    val = np.log(rpf / np.float32(16.0)).astype(np.float32)
    val = (val / np.float32(math.log(128.0 / 16.0)) * np.float32(16.0)).astype(np.float32)
    large = np.minimum(16 + val.astype(np.int32), 31)
    return np.where(is_small, rp, large).astype(np.int64)


def _classify(kt, qc):
    q0 = qc * QCW
    k0 = kt * 128
    if kt < 16:
        dmin, dmax = q0 - (k0 + 127), q0 + QCW - 1 - k0
        if dmax <= 0:
            return ("c0",)
        if dmin >= 113:
            return ("c31",)
        return ("band", 0, q0 - k0 + 4095)
    if q0 - (k0 + 127) + 4096 >= 113:
        return ("pc",)
    return ("band", 32, q0 - k0 + 4095)


_BANDED = [(kt, qc) for qc in range(NQC) for kt in range(NKT)
           if _classify(kt, qc)[0] == "band"]
GW = max(_classify(kt, qc)[2] + QCW for kt, qc in _BANDED) + 1


class Rec:
    def __init__(self):
        self.q = {e: [] for e in ENGS}
        self.meta = {e: [] for e in ENGS}
        self.cnt = {}
        self.cuts = []

    def add(self, e, fn, meta=None):
        self.q[e].append(fn)
        self.meta[e].append(meta)

    def inc(self, name, k=1):
        self.cnt[name] = self.cnt.get(name, 0) + k
        return self.cnt[name]

    def val(self, name):
        return self.cnt.get(name, 0)


SEMNAMES = ["s0", "sgsh", "smm", "sv", "ss", "sph_t", "sph_v", "sph_s",
            "sx0", "sx1", "sw", "swo",
            "sw1a0", "sw1a1", "sw1b0", "sw1b1", "sw2_0", "sw2_1",
            "sbb0", "sbb1", "sbb2", "sbb3",
            "sao", "sat0", "sat1", "srh", "srb", "sac0", "sac1",
            "sgx", "sga", "sout"]


def build(debug=False, cut=None):
    nc = bass.Bass("TRN2", target_bir_lowering=False, debug=False)

    x_d = nc.dram_tensor("x", [SEQ, HID], F32, kind="ExternalInput").ap()
    wq_d = nc.dram_tensor("wq", [HID, HID], F32, kind="ExternalInput").ap()
    wk_d = nc.dram_tensor("wk", [HID, HID], F32, kind="ExternalInput").ap()
    wv_d = nc.dram_tensor("wv", [HID, HID], F32, kind="ExternalInput").ap()
    wo_d = nc.dram_tensor("wo", [HID, HID], F32, kind="ExternalInput").ap()
    wg_d = nc.dram_tensor("wg", [NE, HID], F32, kind="ExternalInput").ap()
    w1_d = nc.dram_tensor("w1", [NE, HID, FF2], F32, kind="ExternalInput").ap()
    w2_d = nc.dram_tensor("w2", [NE, FF, HID], F32, kind="ExternalInput").ap()
    relb_d = nc.dram_tensor("relb", [NB, NB], F32, kind="ExternalInput").ap()
    oho_d = nc.dram_tensor("oho", [NB, GTAB], BF16, kind="ExternalInput").ap()
    ohp_d = nc.dram_tensor("ohp", [NB, GTAB], BF16, kind="ExternalInput").ap()
    idf_d = nc.dram_tensor("identf", [128, 128], F32, kind="ExternalInput").ap()
    out_d = nc.dram_tensor("out", [OWN, HID], F32, kind="ExternalOutput").ap()
    sk = "ExternalOutput" if debug else "Internal"
    g2_d = nc.dram_tensor("g2", [64, GTAB], BF16, kind=sk).ap()
    gsh_d = nc.dram_tensor("gsh", [64 * 128 * GW], BF16, kind="Internal").ap()
    aoT_d = nc.dram_tensor("aoT", [HID, OWN], BF16, kind=sk).ap()
    at32_d = nc.dram_tensor("at32", [OWN, HID], F32, kind=sk).ap()
    rch_d = nc.dram_tensor("rch", [2, QCW], F32, kind="Internal").ap()
    if debug:
        csbg_d = nc.dram_tensor("csbg", [128, NOT_ * 8], F32, kind=sk).ap()

    r = Rec()
    sems = {}

    import contextlib
    with contextlib.ExitStack() as st_:
        for n in SEMNAMES:
            sems[n] = st_.enter_context(nc.semaphore(n))

        xnT = st_.enter_context(nc.sbuf_tensor("xnT", [128, NT * OWN], BF16))
        bigb = st_.enter_context(nc.sbuf_tensor("bigb", [128, 32768], BF16))
        arena = st_.enter_context(nc.sbuf_tensor("arena", [128, 32768], BF16))
        C_sb = st_.enter_context(nc.sbuf_tensor("Csb", [128, NOT_ * 8], F32))
        wgT_hi = st_.enter_context(nc.sbuf_tensor("wgthi", [128, NT * 8], BF16))
        wgT_lo = st_.enter_context(nc.sbuf_tensor("wgtlo", [128, NT * 8], BF16))
        ctab = st_.enter_context(nc.sbuf_tensor("ctab", [128, 96], F32))
        m60 = st_.enter_context(nc.sbuf_tensor("m60", [128, 1], F32))
        idf = st_.enter_context(nc.sbuf_tensor("idf", [128, 128], F32))
        ones_f = st_.enter_context(nc.sbuf_tensor("onesf", [1, 128], F32))
        stats = st_.enter_context(nc.sbuf_tensor("stats", [128, 32], F32))
        dm_sb = st_.enter_context(nc.sbuf_tensor("dm", [128, NOT_], F32))
        eq_sb = st_.enter_context(nc.sbuf_tensor("eq", [128, 2 * NOT_ * 8], BF16))
        w12_sb = st_.enter_context(nc.sbuf_tensor("w12", [128, 2 * NOT_], F32))
        tmp8 = st_.enter_context(nc.sbuf_tensor("tmp8", [128, 8], F32))
        ext3 = st_.enter_context(nc.sbuf_tensor("ext3", [32, 4], F32))
        rowf = st_.enter_context(nc.sbuf_tensor("rowf", [1, 32], F32))
        rc0 = st_.enter_context(nc.sbuf_tensor("rc0", [1, QCW], F32))
        PS = st_.enter_context(nc.psum_tensor("PS", [128, 4096], F32))
        blk = st_.enter_context(nc.Block())

        AR = 32768
        ar_h = arena[:, :].tensor

        def bfv(lo, n):
            return arena[:, lo:lo + n]

        def f32v(lo, n):
            return arena[:, lo:lo + 2 * n].bitcast(F32)

        oh_sb = arena[0:32, 0:GTAB]
        ohp_sb = arena[0:32, 8192:8192 + GTAB]
        gown_sb = arena[32:64, 0:GTAB]
        gpar_sb = arena[32:64, 8192:8192 + GTAB]
        wg_sb = arena[0:8, 16384:20480].bitcast(F32)
        relbf_sb = arena[0:32, 20480:20512]
        relbf32 = arena[0:32, 20512:20576].bitcast(F32)
        xstage = [f32v(0, 2048), f32v(4096, 2048)]
        xnf = [f32v(8192, 2048), f32v(12288, 2048)]
        xnloT = bfv(16384, 2048)
        wq_sb = bfv(0, 2048)
        wk_sb = bfv(2048, 2048)
        wv_sb = bfv(4096, 2048)
        QT2 = bfv(6144, 2048)
        KTz = [bfv(8192, 4096), bfv(12288, 4096)]
        V2 = bfv(16384, NKT * 130)
        abuf = [[bfv(20544 + p * QCW, QCW) for p in range(2)]]
        btile = [bfv(22592 + p * QCW, QCW) for p in range(4)]
        aostage = bfv(26688, QCW)
        rb_sb = f32v(27712, QCW)
        vtf = [f32v(29760 + p * QCW, 512) for p in range(2)]
        aoc = [bfv(0, 2048), bfv(2048, 2048)]
        ostg = [f32v(4096, 2048), f32v(8192, 2048)]
        w1s = [bfv(k * 2048, 2048) for k in range(4)]
        w2c = [bfv(8192 + k * NFT * 512, NFT * 512) for k in range(2)]
        gT = bfv(8192 + 2 * NFT * 512, NFT * QCW)
        sl_sb = f32v(30720, 1024)
        outacc = bigb[:, :].bitcast(F32)

        def W(e, sem, val):
            if val:
                r.add(e, lambda E, h=sems[sem], v=val: E.wait_ge(h, v),
                      ("wait", sem, val))

        def WM(e, mark):
            if mark is not None:
                W(e, mark[0], mark[1])

        def DMA(e, sem, out, in_, **kw):
            v = r.inc(sem, 16)

            def f(E, o=out, i=in_, h=sems[sem], k=dict(kw)):
                E.dma_start(o, i, **k).then_inc(h, 16)
            r.add(e, f, ("inc", sem, 16))
            return v

        def OP(e, sem, f, inc=1):
            if inc:
                v = r.inc(sem, inc)
                r.add(e, lambda E, ff=f, h=sems[sem], k=inc: ff(E).then_inc(h, k),
                      ("inc", sem, inc))
                return v
            r.add(e, lambda E, ff=f: ff(E))
            return r.val(sem)

        def VOP(f):
            return OP("vector", "sv", f)

        def VCH():
            W("vector", "sv", r.val("sv"))

        def MM(out, lhsT, rhs, start, stop, inc=0):
            if inc:
                v = r.inc("smm")
                r.add("tensor", lambda E, o=out, l=lhsT, rr=rhs, s=start, p=stop,
                      h=sems["smm"]: E.matmul(o, l, rr, start=s, stop=p).then_inc(h, 1),
                      ("inc", "smm", 1))
                return v
            r.add("tensor", lambda E, o=out, l=lhsT, rr=rhs, s=start, p=stop:
                  E.matmul(o, l, rr, start=s, stop=p))
            return None

        def barrier():
            r.cuts.append({e: len(r.q[e]) for e in ENGS})
            mk = {"tensor": OP("tensor", "sph_t", lambda E: E.nop()),
                  "vector": OP("vector", "sph_v", lambda E: E.nop()),
                  "scalar": OP("scalar", "sph_s", lambda E: E.nop())}
            pairs = {"tensor": "sph_t", "vector": "sph_v", "scalar": "sph_s"}
            for e in pairs:
                for e2, sem in pairs.items():
                    if e2 != e:
                        W(e, sem, mk[e2])

        DMA("sync", "s0", idf[:, :], idf_d)
        DMA("sync", "s0", relbf32[:, :], relb_d)
        DMA("sync", "s0", wg_sb[:, :], wg_d)
        DMA("sync", "s0", oh_sb[:, :], oho_d)
        v0 = DMA("sync", "s0", ohp_sb[:, :], ohp_d)
        W("vector", "s0", v0)
        VOP(lambda E: E.tensor_copy(relbf_sb[:, :], relbf32[:, :]))
        VOP(lambda E: E.memset(ones_f[:, :], 1.0))
        VOP(lambda E: E.memset(m60[:, :], -60.0))
        v_tab = VOP(lambda E: E.memset(stv(0, 15), 1e-5))

        W("tensor", "sv", v_tab)
        CH = [(c * 512, min(512, GTAB - c * 512)) for c in range(16)]
        extract = {(0, 7): (4095 - 7 * 512, 0), (0, 8): (4208 - 8 * 512, 1),
                   (1, 3): (2000 - 3 * 512, 2)}
        bank_free = [None, None]
        for ri, (ohs, gdst) in enumerate(((oh_sb, gown_sb), (ohp_sb, gpar_sb))):
            for ci, (c0, cw) in enumerate(CH):
                bk = (ri * 16 + ci) % 2
                WM("tensor", bank_free[bk])
                mv = MM(PS[0:32, bk * 512:bk * 512 + cw], relbf_sb[:, :],
                        ohs[:, c0:c0 + cw], True, True, inc=1)
                W("vector", "smm", mv)
                VOP(lambda E, g=gdst, a=c0, wd=cw, b=bk * 512:
                    E.tensor_copy(g[:, a:a + wd], PS[0:32, b:b + wd]))
                if (ri, ci) in extract:
                    off, kx = extract[(ri, ci)]
                    VCH()
                    VOP(lambda E, o=off, kk=kx, b=bk * 512:
                        E.tensor_copy(ext3[:, kk:kk + 1], PS[0:32, b + o:b + o + 1]))
                bank_free[bk] = ("sv", r.val("sv"))
        v_g = r.val("sv")
        W("sync", "sv", v_g)
        DMA("sync", "s0", g2_d[0:32, :], gown_sb[:, :])
        vg2 = DMA("sync", "s0", g2_d[32:64, :], gpar_sb[:, :])
        W("sync", "s0", vg2)
        W("scalar", "s0", r.val("s0"))
        for p in range(128):
            ln = GW - p
            DMA("scalar", "sgsh",
                AP(gsh_d.tensor, p * GW + p, [[128 * GW, 64], [1, ln]]),
                AP(g2_d.tensor, 0, [[GTAB, 64], [1, ln]]))
        gsh_full = r.val("sgsh")
        for kx in range(3):
            tv = OP("tensor", "smm", lambda E, kk=kx:
                    E.transpose(PS[0:1, 1536:1568], ext3[:, kk:kk + 1], idf[0:32, 0:32]))
            W("vector", "smm", tv)
            cv = VOP(lambda E: E.tensor_copy(rowf[:, :], PS[0:1, 1536:1568]))
            W("tensor", "sv", cv)
            bv = OP("tensor", "smm", lambda E:
                    E.matmul(PS[:, 1024:1056], ones_f[:, :], rowf[:, :],
                             start=True, stop=True))
            W("vector", "smm", bv)
            VOP(lambda E, kk=kx: E.tensor_scalar(
                ctab[:, kk * 32:(kk + 1) * 32], PS[:, 1024:1056], -60.0, None, ALU.add))
        for t in range(NT):
            tv = OP("tensor", "smm", lambda E, tt=t:
                    E.transpose(PS[:, 1536 + (tt % 2) * 16:1544 + (tt % 2) * 16],
                                wg_sb[:, tt * 128:(tt + 1) * 128], idf[0:8, 0:8]))
            W("vector", "smm", tv)
            VOP(lambda E, tt=t: E.tensor_copy(
                wgT_hi[:, tt * 8:(tt + 1) * 8],
                PS[:, 1536 + (tt % 2) * 16:1544 + (tt % 2) * 16]))
            VCH()
            vv = VOP(lambda E, tt=t: E.tensor_tensor(
                wgT_lo[:, tt * 8:(tt + 1) * 8],
                PS[:, 1536 + (tt % 2) * 16:1544 + (tt % 2) * 16],
                wgT_hi[:, tt * 8:(tt + 1) * 8], ALU.subtract))
            W("tensor", "sv", vv)

        barrier()

        S_s1, S_mean, S_s2, S_var, S_msq, S_std, S_rstd = range(7)
        S_m1, S_m2 = 8, 9

        def stv(par, k):
            return stats[:, par * 16 + k:par * 16 + k + 1]

        consumed = {}
        slot_ev = [None] * 4
        gate_ps_free = None
        prev_gmm = None
        xn_done = {}
        tr_mark = {}

        def emit_ln_a(xt):
            par = xt % 2
            sxn = "sx0" if par == 0 else "sx1"
            if xt >= 2:
                sm, vm = consumed[xt - 2]
                W("sync", "ss", sm)
                W("sync", "sv", vm)
            vx = DMA("sync", sxn, xstage[par][:, :], x_d[xt * 128:(xt + 1) * 128, :])
            W("scalar", sxn, vx)
            if xt - 2 in tr_mark:
                W("scalar", "smm", tr_mark[xt - 2])
            OP("scalar", "ss", lambda E, p=par: E.activation(
                xnf[p][:, :], xstage[p][:, :], AF.Identity, accum_out=stv(p, S_s1)))
            W("scalar", "ss", r.val("ss"))
            sq_ = OP("scalar", "ss", lambda E, p=par: E.activation(
                xnf[p][:, :], xstage[p][:, :], AF.Square, accum_out=stv(p, S_s2)))
            W("vector", "ss", sq_)
            VOP(lambda E, p=par: E.tensor_scalar(stv(p, S_mean), stv(p, S_s1),
                                                 1.0 / HID, None, ALU.mult))
            VOP(lambda E, p=par: E.tensor_scalar(stv(p, S_var), stv(p, S_s2),
                                                 1.0 / HID, 1e-5, ALU.mult, ALU.add))
            VCH()
            VOP(lambda E, p=par: E.tensor_tensor(stv(p, S_msq), stv(p, S_mean),
                                                 stv(p, S_mean), ALU.mult))
            VCH()
            vv = VOP(lambda E, p=par: E.tensor_tensor(stv(p, S_var), stv(p, S_var),
                                                      stv(p, S_msq), ALU.subtract))
            W("scalar", "sv", vv)
            OP("scalar", "ss", lambda E, p=par: E.activation(
                stv(p, S_std), stv(p, S_var), AF.Ln))
            W("scalar", "ss", r.val("ss"))
            sv_ = OP("scalar", "ss", lambda E, p=par: E.activation(
                stv(p, S_rstd), stv(p, S_std), AF.Exp, scale=-0.5))
            consumed[xt] = (sv_, None)
            return sv_

        def emit_ln_b(xt, sv_):
            par = xt % 2
            W("vector", "ss", sv_)
            vv = VOP(lambda E, p=par: E.tensor_scalar(
                xnf[p][:, :], xstage[p][:, :], stv(p, S_mean), stv(p, S_rstd),
                ALU.subtract, ALU.mult))
            consumed[xt] = (sv_, vv)
            xn_done[xt] = vv

        def emit_tr(xt):
            nonlocal gate_ps_free, prev_gmm
            par = xt % 2
            own = xt < NOT_
            W("tensor", "sv", xn_done[xt])
            if own and prev_gmm is not None:
                W("vector", "smm", prev_gmm)
            dstb = xnT if own else bigb
            dsto = xt * 128 if own else (xt - NOT_) * 128
            for t4 in range(4):
                WM("tensor", slot_ev[0])
                for k4 in range(4):
                    t = t4 * 4 + k4
                    tv = OP("tensor", "smm", lambda E, tt=t, s=k4, p=par: E.transpose(
                        PS[:, s * 512:s * 512 + 128], xnf[p][:, tt * 128:(tt + 1) * 128],
                        idf[:, :]))
                W("vector", "smm", tv)
                ps4 = AP(PS[:, :].tensor, PS[:, :].offset, [[4096, 128], [512, 4], [1, 128]])
                dst4 = AP(dstb[:, :].tensor, dstb[:, :].offset + t4 * 4 * OWN + dsto,
                          [[NT * OWN, 128], [OWN, 4], [1, 128]])
                VOP(lambda E, a=dst4, b=ps4: E.tensor_copy(a, b))
                if own:
                    VCH()
                    lo4 = xnloT[:, t4 * 512:(t4 + 1) * 512]
                    VOP(lambda E, a=lo4, b=ps4, c=dst4: E.tensor_tensor(a, b, c, ALU.subtract))
                slot_ev[0] = ("sv", r.val("sv"))
            if own:
                vlo = r.val("sv")
                W("tensor", "sv", vlo)
                WM("tensor", gate_ps_free)
                n_mm = 0
                for t in range(NT):
                    hi = xnT[:, t * OWN + xt * 128: t * OWN + (xt + 1) * 128]
                    lo = xnloT[:, t * 128:(t + 1) * 128]
                    whi = wgT_hi[:, t * 8:(t + 1) * 8]
                    wlo = wgT_lo[:, t * 8:(t + 1) * 8]
                    for lh, rh in ((hi, whi), (hi, wlo), (lo, whi), (lo, wlo)):
                        n_mm += 1
                        MM(PS[:, 2048:2056], lh, rh, n_mm == 1, n_mm == 64,
                           inc=1 if n_mm == 64 else 0)
                prev_gmm = r.val("smm")
                W("vector", "smm", prev_gmm)
                VOP(lambda E: E.reduce_max(stv(0, S_m1), PS[:, 2048:2056], axis=AX.X))
                VCH()
                VOP(lambda E, q=xt: E.tensor_scalar(
                    eq_sb[:, q * 8:(q + 1) * 8], PS[:, 2048:2056], stv(0, S_m1),
                    None, ALU.is_equal))
                VCH()
                VOP(lambda E, q=xt: E.scalar_tensor_tensor(
                    tmp8[:, :], eq_sb[:, q * 8:(q + 1) * 8], -1e30, PS[:, 2048:2056],
                    ALU.mult, ALU.add))
                VCH()
                VOP(lambda E: E.reduce_max(stv(0, S_m2), tmp8[:, :], axis=AX.X))
                VCH()
                VOP(lambda E, q=xt: E.tensor_tensor(
                    dm_sb[:, q:q + 1], stv(0, S_m1), stv(0, S_m2), ALU.subtract))
                vv = VOP(lambda E, q=xt: E.tensor_scalar(
                    eq_sb[:, (NOT_ + q) * 8:(NOT_ + q + 1) * 8], tmp8[:, :],
                    stv(0, S_m2), None, ALU.is_equal))
                gate_ps_free = ("sv", vv)
            tr_mark[xt] = r.val("smm")

        ln_marks = {}
        for xt in range(NXT):
            ln_marks[xt] = emit_ln_a(xt)
            if xt > 0:
                emit_tr(xt - 1)
            emit_ln_b(xt, ln_marks[xt])
        emit_tr(NXT - 1)
        W("scalar", "sv", r.val("sv"))
        sv_ = OP("scalar", "ss", lambda E: E.activation(
            w12_sb[:, 0:NOT_], dm_sb[:, :], AF.Sigmoid))
        W("vector", "ss", sv_)
        VOP(lambda E: E.tensor_scalar(w12_sb[:, NOT_:2 * NOT_], w12_sb[:, 0:NOT_],
                                      -1.0, 1.0, ALU.mult, ALU.add))
        VCH()
        for q in range(NOT_):
            VOP(lambda E, qq=q: E.tensor_scalar(
                tmp8[:, :], eq_sb[:, qq * 8:(qq + 1) * 8],
                w12_sb[:, qq:qq + 1], None, ALU.mult))
            VCH()
            VOP(lambda E, qq=q: E.scalar_tensor_tensor(
                C_sb[:, qq * 8:(qq + 1) * 8],
                eq_sb[:, (NOT_ + qq) * 8:(NOT_ + qq + 1) * 8],
                w12_sb[:, NOT_ + qq:NOT_ + qq + 1], tmp8[:, :], ALU.mult, ALU.add))
            VCH()
        vC = r.val("sv")
        if debug:
            W("sync", "sv", vC)
            DMA("sync", "s0", csbg_d, C_sb[:, :])

        barrier()
        VOP(lambda E: E.memset(KTz[0][64:128, :], 0.0))
        VOP(lambda E: E.memset(KTz[1][0:64, :], 0.0))
        VOP(lambda E: E.memset(AP(ar_h, 16384 + 64, [[AR, 128], [130, NKT]]), 1.0))
        v_b0 = VOP(lambda E: E.memset(AP(ar_h, 16384 + 129, [[AR, 128], [130, NKT]]), 1.0))
        W("tensor", "sv", v_b0)

        W("sync", "sgsh", gsh_full)

        def wslice(wd, p):
            return [AP(wd.tensor, t0 * 128 * HID + p * 128,
                       [[HID, 128], [128 * HID, 4], [1, 128]]) for t0 in (0, 4, 8, 12)]

        def load_pair_w(p, wait_mark):
            for wd, dst in ((wq_d, wq_sb), (wk_d, wk_sb), (wv_d, wv_sb)):
                for i, src in enumerate(wslice(wd, p)):
                    WM("gpsimd", wait_mark)
                    wait_mark = None
                    DMA("gpsimd", "sw", dst[:, i * 512:(i + 1) * 512], src)
            return r.val("sw")

        w_ready = load_pair_w(0, ("sv", vC))
        exp_by_slot = [0, 0]
        av_by_abuf = {}
        psA_free = [None, None]
        psS_free = [None, None]
        vproj_end = {}
        sao_last = 0
        srb_last = 0
        srh_last = 0
        ao_mult_last = None
        bias_slot_free = [None, None, None, None]

        for p in range(NPAIR):
            W("tensor", "sw", w_ready)
            W("tensor", "ss", max(exp_by_slot))
            for t in range(NT):
                for c4 in range(4):
                    MM(PS[:, c4 * 512:(c4 + 1) * 512], wq_sb[:, t * 128:(t + 1) * 128],
                       xnT[:, t * OWN + c4 * 512: t * OWN + (c4 + 1) * 512],
                       t == 0, t == NT - 1,
                       inc=1 if (t == NT - 1 and c4 == 3) else 0)
            qmm = r.val("smm")
            W("vector", "smm", qmm)
            VOP(lambda E: E.tensor_copy(QT2[:, :], PS[:, 0:2048]))
            kv_ev = [None, None]
            WM("tensor", psA_free[0])
            WM("tensor", psA_free[1])
            for kc in range(8):
                bk = 2048 + (kc % 2) * 512
                src_blk = xnT if kc < 4 else bigb
                WM("tensor", kv_ev[kc % 2])
                for t in range(NT):
                    MM(PS[:, bk:bk + 512], wk_sb[:, t * 128:(t + 1) * 128],
                       src_blk[:, t * OWN + (kc % 4) * 512: t * OWN + (kc % 4 + 1) * 512],
                       t == 0, t == NT - 1, inc=1 if t == NT - 1 else 0)
                kmm = r.val("smm")
                W("vector", "smm", kmm)
                VOP(lambda E, b=bk, c=kc: E.tensor_copy(
                    KTz[0][0:64, c * 512:(c + 1) * 512], PS[0:64, b:b + 512]))
                VOP(lambda E, b=bk, c=kc: E.tensor_copy(
                    KTz[1][64:128, c * 512:(c + 1) * 512], PS[64:128, b:b + 512]))
                kv_ev[kc % 2] = ("sv", r.val("sv"))
            tr_ev = [psA_free[1], None]
            psA_free[1] = None
            for kc in range(8):
                bk = 2048 + (kc % 2) * 512
                src_blk = xnT if kc < 4 else bigb
                WM("tensor", kv_ev[kc % 2])
                for t in range(NT):
                    MM(PS[:, bk:bk + 512], wv_sb[:, t * 128:(t + 1) * 128],
                       src_blk[:, t * OWN + (kc % 4) * 512: t * OWN + (kc % 4 + 1) * 512],
                       t == 0, t == NT - 1, inc=1 if t == NT - 1 else 0)
                vmm = r.val("smm")
                W("vector", "smm", vmm)
                VOP(lambda E, b=bk, c=kc: E.tensor_copy(vtf[c % 2][:, :], PS[:, b:b + 512]))
                kv_ev[kc % 2] = ("sv", r.val("sv"))
                W("tensor", "sv", kv_ev[kc % 2][1])
                for k4 in range(4):
                    kt = kc * 4 + k4
                    to = 3072 + (k4 % 2) * 512
                    WM("tensor", tr_ev[k4 % 2])
                    tv = OP("tensor", "smm", lambda E, c=kc, kk=k4, o=to: E.transpose(
                        PS[:, o:o + 128], vtf[c % 2][:, kk * 128:(kk + 1) * 128],
                        idf[:, :]))
                    W("vector", "smm", tv)
                    VOP(lambda E, c=kt, o=to: E.tensor_copy(
                        V2[:, c * 130:c * 130 + 64], PS[:, o:o + 64]))
                    VOP(lambda E, c=kt, o=to: E.tensor_copy(
                        V2[:, c * 130 + 65:c * 130 + 129], PS[:, o + 64:o + 128]))
                    tr_ev[k4 % 2] = ("sv", r.val("sv"))
            vproj_end[p] = r.val("smm")
            if p + 1 < NPAIR:
                w_ready = load_pair_w(p + 1, ("smm", vproj_end[p]))
            else:
                for t in range(NT):
                    WM("gpsimd", ("smm", vproj_end[p]) if t == 0 else None)
                    DMA("gpsimd", "swo", bigb[:, t * 2048:(t + 1) * 2048],
                        AP(wo_d.tensor, t * 128 * HID, [[HID, 128], [1, HID]]))
            W("tensor", "sv", r.val("sv"))

            for h in range(2):
                hg = 2 * p + h
                for qc in range(NQC):
                    aslot = qc % 2
                    psA0 = 2048 + aslot * 1024

                    def emit_S(kt):
                        cls = _classify(kt, qc)
                        sslot = kt % 2
                        psS = PS[:, sslot * 1024:(sslot + 1) * 1024]
                        W("tensor", "ss", exp_by_slot[sslot])
                        for j in range(2):
                            MM(PS[:, sslot * 1024 + j * 512:sslot * 1024 + (j + 1) * 512],
                               KTz[h][:, kt * 128:(kt + 1) * 128],
                               QT2[:, qc * 1024 + j * 512:qc * 1024 + (j + 1) * 512],
                               True, True, inc=1 if j == 1 else 0)
                        smm_v = r.val("smm")
                        bias_m = None
                        if cls[0] == "band":
                            row = cls[1] + hg
                            i0 = cls[2]
                            bslot = kt % 4
                            sbn = "sbb%d" % bslot
                            WM("sync", bias_slot_free[bslot])
                            bias_slot_free[bslot] = None
                            vb = DMA("sync", sbn, btile[bslot][:, :],
                                     AP(gsh_d.tensor, row * 128 * GW + i0,
                                        [[GW, 128], [1, QCW]]))
                            W("vector", sbn, vb)
                            W("vector", "smm", smm_v)
                            bias_m = VOP(lambda E, ps=psS, bt=btile[bslot]:
                                         E.tensor_tensor(ps, ps, bt, ALU.add))
                            bcol = m60[:, 0:1]
                        elif cls[0] == "c0":
                            bcol = ctab[:, hg:hg + 1]
                        elif cls[0] == "c31":
                            bcol = ctab[:, 32 + hg:32 + hg + 1]
                        else:
                            bcol = ctab[:, 64 + hg:64 + hg + 1]
                        if bias_m is not None:
                            W("scalar", "sv", bias_m)
                        else:
                            W("scalar", "smm", smm_v)
                        am = av_by_abuf.get(kt % 2)
                        if am is not None:
                            W("scalar", "smm", am)
                        ev = OP("scalar", "ss", lambda E, ps=psS,
                                ab=abuf[0][kt % 2], bc=bcol:
                                E.activation(ab, ps, AF.Exp, bias=bc))
                        exp_by_slot[sslot] = ev
                        if bias_m is not None:
                            bias_slot_free[bslot] = ("ss", ev)

                    def emit_AV(kt):
                        W("tensor", "ss", exp_by_slot[kt % 2])
                        if kt == 0:
                            WM("tensor", psA_free[aslot])
                            psA_free[aslot] = None
                        for j in range(2):
                            MM(PS[0:65, psA0 + j * 512:psA0 + (j + 1) * 512],
                               V2[:, kt * 130 + h * 65:kt * 130 + h * 65 + 65],
                               abuf[0][kt % 2][:, j * 512:(j + 1) * 512],
                               kt == 0, kt == NKT - 1, inc=1 if j == 1 else 0)
                        av_by_abuf[kt % 2] = r.val("smm")

                    for kt in range(NKT):
                        emit_S(kt)
                        if kt > 0:
                            emit_AV(kt - 1)
                    emit_AV(NKT - 1)
                    avl = av_by_abuf[(NKT - 1) % 2]
                    W("vector", "smm", avl)
                    WM("vector", ao_mult_last)
                    vr = VOP(lambda E, o=psA0: E.reciprocal(
                        rb_sb[64:65, :], PS[64:65, o:o + 1024]))
                    W("sync", "sv", vr)
                    W("sync", "srb", srb_last)
                    rsl = (2 * hg + qc) % 2
                    srh_last = DMA("sync", "srh", rch_d[rsl:rsl + 1, :], rb_sb[64:65, :])
                    W("sync", "srh", srh_last)
                    srb_last = DMA("sync", "srb", rb_sb[0:64, :],
                                   AP(rch_d.tensor, rsl * QCW, [[0, 64], [1, QCW]]))
                    W("vector", "srb", srb_last)
                    if sao_last:
                        W("vector", "sao", sao_last)
                    vao = VOP(lambda E, o=psA0: E.tensor_tensor(
                        aostage[0:64, :], PS[0:64, o:o + 1024],
                        rb_sb[0:64, :], ALU.mult))
                    ao_mult_last = ("sv", vao)
                    psA_free[aslot] = ("sv", vao)
                    W("sync", "sv", vao)
                    sao_last = DMA("sync", "sao",
                                   aoT_d[hg * 64:(hg + 1) * 64, qc * 1024:(qc + 1) * 1024],
                                   aostage[0:64, :])

        barrier()

        r.cuts.append({e: len(r.q[e]) for e in ENGS})
        if not os.environ.get("K_PC_NOSAO"):
            W("sync", "sao", r.val("sao"))
        if not os.environ.get("K_PC_NOWO"):
            W("tensor", "swo", r.val("swo"))
        o_ev = [None, None]
        o_mm_mark = [0, 0]
        sat_hist = {}
        for tt in range(NOT_):
            par = tt % 2
            sem = "sac0" if par == 0 else "sac1"
            if tt >= 2:
                W("sync", "smm", o_mm_mark[par])
            vl = 0
            if not os.environ.get("K_PC_NOLD"):
                for t0 in (0, 4, 8, 12):
                    vl = DMA("sync", sem, aoc[par][:, t0 * 128:(t0 + 4) * 128],
                             AP(aoT_d.tensor, t0 * 128 * OWN + tt * 128,
                                [[OWN, 128], [128 * OWN, 4], [1, 128]]))
                W("tensor", sem, vl)
            WM("tensor", o_ev[par])
            if os.environ.get("K_PC_NOMM"):
                o_mm_mark[par] = OP("tensor", "smm", lambda E: E.nop())
            else:
                for t in range(NT):
                    for hc in range(4):
                        MM(PS[:, par * 2048 + hc * 512:par * 2048 + (hc + 1) * 512],
                           aoc[par][:, t * 128:(t + 1) * 128],
                           bigb[:, t * 2048 + hc * 512:t * 2048 + (hc + 1) * 512],
                           t == 0, t == NT - 1,
                           inc=1 if (t == NT - 1 and hc == 3) else 0)
                o_mm_mark[par] = r.val("smm")
            W("vector", "smm", o_mm_mark[par])
            satn = "sat0" if par == 0 else "sat1"
            if tt >= 2 and sat_hist.get(tt - 2):
                W("vector", satn, sat_hist[tt - 2])
            if os.environ.get("K_PC_NOST"):
                vv = VOP(lambda E: E.nop())
                o_ev[par] = ("sv", vv)
                sat_hist[tt] = 0
            else:
                vv = VOP(lambda E, pp=par: E.tensor_copy(ostg[pp][:, :],
                                                         PS[:, pp * 2048:(pp + 1) * 2048]))
                o_ev[par] = ("sv", vv)
                W("sync", "sv", vv)
                sat_hist[tt] = DMA("sync", satn, at32_d[tt * 128:(tt + 1) * 128, :],
                                   ostg[par][:, :])

        barrier()

        omm_last = r.val("smm")
        m_gTm = {}
        m_sttbank = [None] * 4
        m_f1 = {}
        stt_prev = {}
        sl_free = None
        for half in range(2):
            W("gpsimd", "sat0", r.val("sat0"))
            W("gpsimd", "sat1", r.val("sat1"))
            W("gpsimd", "smm", omm_last)
            if half == 1:
                W("gpsimd", "sout", 16)
            for tt in range(8):
                DMA("gpsimd", "sgx", outacc[:, tt * 2048:(tt + 1) * 2048],
                    x_d[half * 1024 + tt * 128: half * 1024 + (tt + 1) * 128, :])
            W("gpsimd", "sgx", r.val("sgx"))
            for tt in range(8):
                DMA("gpsimd", "sga", outacc[:, tt * 2048:(tt + 1) * 2048],
                    at32_d[half * 1024 + tt * 128: half * 1024 + (tt + 1) * 128, :],
                    accum_op=ALU.add)
            sga_full = r.val("sga")
            for e in range(NE):
                for j in range(NFT):
                    gi = half * NE * NFT + e * NFT + j
                    slot = gi % 2
                    wm = m_f1.get(gi - 2)
                    for ab, foff, sname in ((0, 0, "sw1a%d" % slot),
                                            (2, FF, "sw1b%d" % slot)):
                        WM("gpsimd", ("smm", wm) if wm else None)
                        wm = None
                        DMA("gpsimd", sname, w1s[ab + slot][:, :],
                            AP(w1_d.tensor, e * HID * FF2 + foff + j * 128,
                               [[FF2, 128], [128 * FF2, NT], [1, 128]]))
                    a_ps = 0 if slot == 0 else 2048
                    b_ps = 1024 if slot == 0 else 3072
                    W("tensor", "sw1a%d" % slot, r.val("sw1a%d" % slot))
                    W("tensor", "sw1b%d" % slot, r.val("sw1b%d" % slot))
                    WM("tensor", m_gTm.get(gi - 2))
                    if slot == 0:
                        for bq in range(4):
                            WM("tensor", m_sttbank[bq])
                            m_sttbank[bq] = None
                    for ab, ps0 in ((0, a_ps), (2, b_ps)):
                        for t in range(NT):
                            for c2 in range(2):
                                MM(PS[:, ps0 + c2 * 512:ps0 + (c2 + 1) * 512],
                                   w1s[ab + slot][:, t * 128:(t + 1) * 128],
                                   xnT[:, t * OWN + half * 1024 + c2 * 512:
                                       t * OWN + half * 1024 + (c2 + 1) * 512],
                                   t == 0, t == NT - 1,
                                   inc=1 if (t == NT - 1 and c2 == 1) else 0)
                    m_f1[gi] = r.val("smm")
                    W("scalar", "smm", m_f1[gi])
                    WM("scalar", sl_free)
                    sv_ = OP("scalar", "ss", lambda E, a=a_ps: E.activation(
                        sl_sb[:, :], PS[:, a:a + 1024], AF.Silu))
                    W("vector", "ss", sv_)
                    vv = VOP(lambda E, b=b_ps, jj=j: E.tensor_tensor(
                        gT[:, jj * QCW:(jj + 1) * QCW], sl_sb[:, :],
                        PS[:, b:b + 1024], ALU.mult))
                    m_gTm[gi] = ("sv", vv)
                    sl_free = ("sv", vv)
                gT_ready = m_gTm[half * NE * NFT + e * NFT + NFT - 1]
                for hc in range(4):
                    g2i = half * NE * 4 + e * 4 + hc
                    slot = g2i % 2
                    wfree = m_f1.get(gi) if g2i < 2 else None
                    w2war = stt_prev.get(("w2", slot))
                    WM("gpsimd", w2war)
                    DMA("gpsimd", "sw2_%d" % slot, w2c[slot][:, :],
                        AP(w2_d.tensor, e * FF * HID + hc * 512,
                           [[HID, 128], [128 * HID, NFT], [1, 512]]))
                    w2v = r.val("sw2_%d" % slot)
                    W("tensor", "sw2_%d" % slot, w2v)
                    WM("tensor", gT_ready)
                    gT_ready = None
                    for tt in range(8):
                        bk = tt % 4
                        WM("tensor", m_sttbank[bk])
                        m_sttbank[bk] = None
                        for ft in range(NFT):
                            MM(PS[:, bk * 512:(bk + 1) * 512],
                               gT[:, ft * QCW + tt * 128:ft * QCW + (tt + 1) * 128],
                               w2c[slot][:, ft * 512:(ft + 1) * 512],
                               ft == 0, ft == NFT - 1,
                               inc=1 if ft == NFT - 1 else 0)
                        ymm = r.val("smm")
                        W("vector", "smm", ymm)
                        W("vector", "sga", sga_full)
                        pk = (half, tt, hc)
                        if pk in stt_prev:
                            W("vector", "sv", stt_prev[pk])
                        vv = VOP(lambda E, t_=tt, h_=hc, b=bk, ee=e, hf=half:
                                 E.scalar_tensor_tensor(
                                     outacc[:, t_ * 2048 + h_ * 512:
                                            t_ * 2048 + (h_ + 1) * 512],
                                     PS[:, b * 512:(b + 1) * 512],
                                     C_sb[:, (hf * 8 + t_) * 8 + ee:
                                          (hf * 8 + t_) * 8 + ee + 1],
                                     outacc[:, t_ * 2048 + h_ * 512:
                                            t_ * 2048 + (h_ + 1) * 512],
                                     ALU.mult, ALU.add))
                        stt_prev[pk] = vv
                        m_sttbank[bk] = ("sv", vv)
                    stt_prev[("w2", slot)] = ("smm", r.val("smm"))
            W("sync", "sv", r.val("sv"))
            DMA("sync", "sout",
                AP(out_d.tensor, half * 1024 * HID,
                   [[HID, 128], [128 * HID, 8], [1, HID]]),
                outacc[:, :].rearrange("p (t c) -> p t c", t=8))
        W("sync", "sout", r.val("sout"))

        for e in ENGS:
            n = len(r.q[e]) if cut is None else r.cuts[cut][e]
            getattr(blk, e)(lambda E, Q=tuple(r.q[e][:n]): [f(E) for f in Q])

    build.last_rec = r
    return nc


_NC_CACHE = {}


def _get_nc(debug=False):
    if debug not in _NC_CACHE:
        _NC_CACHE[debug] = build(debug)
    return _NC_CACHE[debug]


def _onehot_tables(half):
    i = np.arange(GTAB, dtype=np.int64)
    d_own = i - 4095
    d_par = d_own if half == 0 else d_own + 4096
    oh_o = np.zeros((NB, GTAB), dtype=ml_dtypes.bfloat16)
    oh_p = np.zeros((NB, GTAB), dtype=ml_dtypes.bfloat16)
    oh_o[_bucket_np(d_own), i] = 1
    oh_p[_bucket_np(d_par), i] = 1
    return oh_o, oh_p


def _make_inputs(inputs):
    x = np.asarray(inputs["L0_attn_hidden_states"])
    common = {
        "wq": np.ascontiguousarray(inputs["L0_attn_Wq"], dtype=np.float32),
        "wk": np.ascontiguousarray(inputs["L0_attn_Wk"], dtype=np.float32),
        "wv": np.ascontiguousarray(inputs["L0_attn_Wv"], dtype=np.float32),
        "wo": np.ascontiguousarray(inputs["L0_attn_Wo"], dtype=np.float32),
        "relb": np.ascontiguousarray(inputs["L0_attn_rel_bias"], dtype=np.float32),
        "wg": np.ascontiguousarray(inputs["L0_ffn_wg"], dtype=np.float32),
        "w1": np.ascontiguousarray(inputs["L0_ffn_w1"], dtype=np.float32),
        "w2": np.ascontiguousarray(inputs["L0_ffn_w2"], dtype=np.float32),
        "identf": np.eye(128, dtype=np.float32),
    }
    ohs = [_onehot_tables(h) for h in (0, 1)]
    in_maps = []
    for c in range(8):
        b, half = c // 2, c % 2
        xr = np.ascontiguousarray(np.concatenate(
            [x[b, half * OWN:(half + 1) * OWN],
             x[b, (1 - half) * OWN:(2 - half) * OWN]], axis=0).astype(np.float32))
        m = dict(common)
        m["x"] = xr
        m["oho"], m["ohp"] = ohs[half]
        in_maps.append(m)
    return in_maps


def kernel(**inputs):
    nc = _get_nc(False)
    in_maps = _make_inputs(inputs)
    res = run_bass_kernel_spmd(nc, in_maps, core_ids=list(range(8)))
    out = np.empty((4, SEQ, HID), dtype=np.float32)
    for c in range(8):
        b, half = c // 2, c % 2
        out[b, half * OWN:(half + 1) * OWN] = res.results[c]["out"]
    return out


# version 09
# speedup vs baseline: 1.1797x; 1.0988x over previous
import math
import os
import numpy as np
import ml_dtypes
import concourse.bass as bass
import concourse.mybir as mybir
from concourse.ap import AP
from concourse.bass_utils import run_bass_kernel_spmd

F32 = mybir.dt.float32
BF16 = mybir.dt.bfloat16
AF = mybir.ActivationFunctionType
ALU = mybir.AluOpType
AX = mybir.AxisListType

HID = 2048
SEQ = 4096
OWN = 2048
NT = 16
NXT = 32
NOT_ = 16
NPAIR = 16
NKT = 32
QCW = 1024
NQC = 2
FF = 1408
NFT = 11
FF2 = 2816
NE = 8
NB = 32
GTAB = 8191

ENGS = ("sync", "gpsimd", "tensor", "vector", "scalar")


def _bucket_np(d):
    rp = np.maximum(d, 0).astype(np.int64)
    is_small = rp < 16
    rpf = np.maximum(rp, 1).astype(np.float32)
    val = np.log(rpf / np.float32(16.0)).astype(np.float32)
    val = (val / np.float32(math.log(128.0 / 16.0)) * np.float32(16.0)).astype(np.float32)
    large = np.minimum(16 + val.astype(np.int32), 31)
    return np.where(is_small, rp, large).astype(np.int64)


def _classify(kt, qc):
    q0 = qc * QCW
    k0 = kt * 128
    if kt < 16:
        dmin, dmax = q0 - (k0 + 127), q0 + QCW - 1 - k0
        if dmax <= 0:
            return ("c0",)
        if dmin >= 113:
            return ("c31",)
        return ("band", 0, q0 - k0 + 4095)
    if q0 - (k0 + 127) + 4096 >= 113:
        return ("pc",)
    return ("band", 32, q0 - k0 + 4095)


_BANDED = [(kt, qc) for qc in range(NQC) for kt in range(NKT)
           if _classify(kt, qc)[0] == "band"]
GW = max(_classify(kt, qc)[2] + QCW for kt, qc in _BANDED) + 1


class Rec:
    def __init__(self):
        self.q = {e: [] for e in ENGS}
        self.meta = {e: [] for e in ENGS}
        self.cnt = {}
        self.cuts = []

    def add(self, e, fn, meta=None):
        self.q[e].append(fn)
        self.meta[e].append(meta)

    def inc(self, name, k=1):
        self.cnt[name] = self.cnt.get(name, 0) + k
        return self.cnt[name]

    def val(self, name):
        return self.cnt.get(name, 0)


SEMNAMES = ["s0", "sgsh", "smm", "sv", "ss", "sph_t", "sph_v", "sph_s",
            "sx0", "sx1", "sw", "swo",
            "sw1a0", "sw1a1", "sw1b0", "sw1b1", "sw2_0", "sw2_1",
            "sbb0", "sbb1", "sbb2", "sbb3",
            "sao", "sat0", "sat1", "srh", "srb", "sac0", "sac1",
            "sgx", "sga", "sout"]


def build(debug=False, cut=None):
    nc = bass.Bass("TRN2", target_bir_lowering=False, debug=False)

    x_d = nc.dram_tensor("x", [SEQ, HID], F32, kind="ExternalInput").ap()
    wq_d = nc.dram_tensor("wq", [HID, HID], F32, kind="ExternalInput").ap()
    wk_d = nc.dram_tensor("wk", [HID, HID], F32, kind="ExternalInput").ap()
    wv_d = nc.dram_tensor("wv", [HID, HID], F32, kind="ExternalInput").ap()
    wo_d = nc.dram_tensor("wo", [HID, HID], F32, kind="ExternalInput").ap()
    wg_d = nc.dram_tensor("wg", [NE, HID], F32, kind="ExternalInput").ap()
    w1_d = nc.dram_tensor("w1", [NE, HID, FF2], F32, kind="ExternalInput").ap()
    w2_d = nc.dram_tensor("w2", [NE, FF, HID], F32, kind="ExternalInput").ap()
    relb_d = nc.dram_tensor("relb", [NB, NB], F32, kind="ExternalInput").ap()
    oho_d = nc.dram_tensor("oho", [NB, GTAB], BF16, kind="ExternalInput").ap()
    ohp_d = nc.dram_tensor("ohp", [NB, GTAB], BF16, kind="ExternalInput").ap()
    idf_d = nc.dram_tensor("identf", [128, 128], F32, kind="ExternalInput").ap()
    out_d = nc.dram_tensor("out", [OWN, HID], F32, kind="ExternalOutput").ap()
    sk = "ExternalOutput" if debug else "Internal"
    g2_d = nc.dram_tensor("g2", [64, GTAB], BF16, kind=sk).ap()
    gsh_d = nc.dram_tensor("gsh", [64 * 128 * GW], BF16, kind="Internal").ap()
    aoT_d = nc.dram_tensor("aoT", [HID, OWN], BF16, kind=sk).ap()
    at32_d = nc.dram_tensor("at32", [OWN, HID], F32, kind=sk).ap()
    rch_d = nc.dram_tensor("rch", [2, QCW], F32, kind="Internal").ap()
    if debug:
        csbg_d = nc.dram_tensor("csbg", [128, NOT_ * 8], F32, kind=sk).ap()

    r = Rec()
    sems = {}

    import contextlib
    with contextlib.ExitStack() as st_:
        for n in SEMNAMES:
            sems[n] = st_.enter_context(nc.semaphore(n))

        xnT = st_.enter_context(nc.sbuf_tensor("xnT", [128, NT * OWN], BF16))
        bigb = st_.enter_context(nc.sbuf_tensor("bigb", [128, 32768], BF16))
        arena = st_.enter_context(nc.sbuf_tensor("arena", [128, 32768], BF16))
        C_sb = st_.enter_context(nc.sbuf_tensor("Csb", [128, NOT_ * 8], F32))
        wgT_hi = st_.enter_context(nc.sbuf_tensor("wgthi", [128, NT * 8], BF16))
        wgT_lo = st_.enter_context(nc.sbuf_tensor("wgtlo", [128, NT * 8], BF16))
        ctab = st_.enter_context(nc.sbuf_tensor("ctab", [128, 96], F32))
        m60 = st_.enter_context(nc.sbuf_tensor("m60", [128, 1], F32))
        idf = st_.enter_context(nc.sbuf_tensor("idf", [128, 128], F32))
        ones_f = st_.enter_context(nc.sbuf_tensor("onesf", [1, 128], F32))
        stats = st_.enter_context(nc.sbuf_tensor("stats", [128, 32], F32))
        dm_sb = st_.enter_context(nc.sbuf_tensor("dm", [128, NOT_], F32))
        eq_sb = st_.enter_context(nc.sbuf_tensor("eq", [128, 2 * NOT_ * 8], BF16))
        w12_sb = st_.enter_context(nc.sbuf_tensor("w12", [128, 2 * NOT_], F32))
        tmp8 = st_.enter_context(nc.sbuf_tensor("tmp8", [128, 8], F32))
        ext3 = st_.enter_context(nc.sbuf_tensor("ext3", [32, 4], F32))
        rowf = st_.enter_context(nc.sbuf_tensor("rowf", [1, 32], F32))
        rc0 = st_.enter_context(nc.sbuf_tensor("rc0", [1, QCW], F32))
        PS = st_.enter_context(nc.psum_tensor("PS", [128, 4096], F32))
        blk = st_.enter_context(nc.Block())

        AR = 32768
        ar_h = arena[:, :].tensor

        def bfv(lo, n):
            return arena[:, lo:lo + n]

        def f32v(lo, n):
            return arena[:, lo:lo + 2 * n].bitcast(F32)

        oh_sb = arena[0:32, 0:GTAB]
        ohp_sb = arena[0:32, 8192:8192 + GTAB]
        gown_sb = arena[32:64, 0:GTAB]
        gpar_sb = arena[32:64, 8192:8192 + GTAB]
        wg_sb = arena[0:8, 16384:20480].bitcast(F32)
        relbf_sb = arena[0:32, 20480:20512]
        relbf32 = arena[0:32, 20512:20576].bitcast(F32)
        xstage = [f32v(0, 2048), f32v(4096, 2048)]
        xnf = [f32v(8192, 2048), f32v(12288, 2048)]
        xnloT = bfv(16384, 2048)
        wq_sb = bfv(0, 2048)
        wk_sb = bfv(2048, 2048)
        wv_sb = bfv(4096, 2048)
        QT2 = bfv(6144, 2048)
        KTz = [bfv(8192, 4096), bfv(12288, 4096)]
        V2 = bfv(16384, NKT * 130)
        abuf = [[bfv(20544 + p * QCW, QCW) for p in range(2)]]
        btile = [bfv(22592 + p * QCW, QCW) for p in range(4)]
        aostage = bfv(26688, QCW)
        rb_sb = f32v(27712, QCW)
        vtf = [f32v(29760 + p * QCW, 512) for p in range(2)]
        aoc = [bfv(0, 2048), bfv(2048, 2048)]
        ostg = [f32v(4096, 2048), f32v(8192, 2048)]
        w1s = [bfv(k * 2048, 2048) for k in range(4)]
        w2c = [bfv(8192 + k * NFT * 512, NFT * 512) for k in range(2)]
        gT = bfv(8192 + 2 * NFT * 512, NFT * QCW)
        sl_sb = f32v(30720, 1024)
        outacc = bigb[:, :].bitcast(F32)

        def W(e, sem, val):
            if val:
                r.add(e, lambda E, h=sems[sem], v=val: E.wait_ge(h, v),
                      ("wait", sem, val))

        def WM(e, mark):
            if mark is not None:
                W(e, mark[0], mark[1])

        def DMA(e, sem, out, in_, **kw):
            v = r.inc(sem, 16)

            def f(E, o=out, i=in_, h=sems[sem], k=dict(kw)):
                E.dma_start(o, i, **k).then_inc(h, 16)
            r.add(e, f, ("inc", sem, 16))
            return v

        def OP(e, sem, f, inc=1):
            if inc:
                v = r.inc(sem, inc)
                r.add(e, lambda E, ff=f, h=sems[sem], k=inc: ff(E).then_inc(h, k),
                      ("inc", sem, inc))
                return v
            r.add(e, lambda E, ff=f: ff(E))
            return r.val(sem)

        def VOP(f):
            return OP("vector", "sv", f)

        def VCH():
            W("vector", "sv", r.val("sv"))

        def MM(out, lhsT, rhs, start, stop, inc=0):
            if inc:
                v = r.inc("smm")
                r.add("tensor", lambda E, o=out, l=lhsT, rr=rhs, s=start, p=stop,
                      h=sems["smm"]: E.matmul(o, l, rr, start=s, stop=p).then_inc(h, 1),
                      ("inc", "smm", 1))
                return v
            r.add("tensor", lambda E, o=out, l=lhsT, rr=rhs, s=start, p=stop:
                  E.matmul(o, l, rr, start=s, stop=p))
            return None

        def barrier():
            r.cuts.append({e: len(r.q[e]) for e in ENGS})
            mk = {"tensor": OP("tensor", "sph_t", lambda E: E.nop()),
                  "vector": OP("vector", "sph_v", lambda E: E.nop()),
                  "scalar": OP("scalar", "sph_s", lambda E: E.nop())}
            pairs = {"tensor": "sph_t", "vector": "sph_v", "scalar": "sph_s"}
            for e in pairs:
                for e2, sem in pairs.items():
                    if e2 != e:
                        W(e, sem, mk[e2])

        DMA("sync", "s0", idf[:, :], idf_d)
        DMA("sync", "s0", relbf32[:, :], relb_d)
        DMA("sync", "s0", wg_sb[:, :], wg_d)
        DMA("sync", "s0", oh_sb[:, :], oho_d)
        v0 = DMA("sync", "s0", ohp_sb[:, :], ohp_d)
        W("vector", "s0", v0)
        VOP(lambda E: E.tensor_copy(relbf_sb[:, :], relbf32[:, :]))
        VOP(lambda E: E.memset(ones_f[:, :], 1.0))
        VOP(lambda E: E.memset(m60[:, :], -60.0))
        v_tab = VOP(lambda E: E.memset(stv(0, 15), 1e-5))

        W("tensor", "sv", v_tab)
        CH = [(c * 512, min(512, GTAB - c * 512)) for c in range(16)]
        extract = {(0, 7): (4095 - 7 * 512, 0), (0, 8): (4208 - 8 * 512, 1),
                   (1, 3): (2000 - 3 * 512, 2)}
        bank_free = [None, None]
        for ri, (ohs, gdst) in enumerate(((oh_sb, gown_sb), (ohp_sb, gpar_sb))):
            for ci, (c0, cw) in enumerate(CH):
                bk = (ri * 16 + ci) % 2
                WM("tensor", bank_free[bk])
                mv = MM(PS[0:32, bk * 512:bk * 512 + cw], relbf_sb[:, :],
                        ohs[:, c0:c0 + cw], True, True, inc=1)
                W("vector", "smm", mv)
                VOP(lambda E, g=gdst, a=c0, wd=cw, b=bk * 512:
                    E.tensor_copy(g[:, a:a + wd], PS[0:32, b:b + wd]))
                if (ri, ci) in extract:
                    off, kx = extract[(ri, ci)]
                    VCH()
                    VOP(lambda E, o=off, kk=kx, b=bk * 512:
                        E.tensor_copy(ext3[:, kk:kk + 1], PS[0:32, b + o:b + o + 1]))
                bank_free[bk] = ("sv", r.val("sv"))
        v_g = r.val("sv")
        W("sync", "sv", v_g)
        DMA("sync", "s0", g2_d[0:32, :], gown_sb[:, :])
        vg2 = DMA("sync", "s0", g2_d[32:64, :], gpar_sb[:, :])
        W("sync", "s0", vg2)
        W("gpsimd", "s0", r.val("s0"))
        for p in range(128):
            ln = GW - p
            DMA("gpsimd", "sgsh",
                AP(gsh_d.tensor, p * GW + p, [[128 * GW, 64], [1, ln]]),
                AP(g2_d.tensor, 0, [[GTAB, 64], [1, ln]]))
        gsh_full = r.val("sgsh")
        for kx in range(3):
            tv = OP("tensor", "smm", lambda E, kk=kx:
                    E.transpose(PS[0:1, 1536:1568], ext3[:, kk:kk + 1], idf[0:32, 0:32]))
            W("vector", "smm", tv)
            cv = VOP(lambda E: E.tensor_copy(rowf[:, :], PS[0:1, 1536:1568]))
            W("tensor", "sv", cv)
            bv = OP("tensor", "smm", lambda E:
                    E.matmul(PS[:, 1024:1056], ones_f[:, :], rowf[:, :],
                             start=True, stop=True))
            W("vector", "smm", bv)
            VOP(lambda E, kk=kx: E.tensor_scalar(
                ctab[:, kk * 32:(kk + 1) * 32], PS[:, 1024:1056], -60.0, None, ALU.add))
        for t in range(NT):
            tv = OP("tensor", "smm", lambda E, tt=t:
                    E.transpose(PS[:, 1536 + (tt % 2) * 16:1544 + (tt % 2) * 16],
                                wg_sb[:, tt * 128:(tt + 1) * 128], idf[0:8, 0:8]))
            W("vector", "smm", tv)
            VOP(lambda E, tt=t: E.tensor_copy(
                wgT_hi[:, tt * 8:(tt + 1) * 8],
                PS[:, 1536 + (tt % 2) * 16:1544 + (tt % 2) * 16]))
            VCH()
            vv = VOP(lambda E, tt=t: E.tensor_tensor(
                wgT_lo[:, tt * 8:(tt + 1) * 8],
                PS[:, 1536 + (tt % 2) * 16:1544 + (tt % 2) * 16],
                wgT_hi[:, tt * 8:(tt + 1) * 8], ALU.subtract))
            W("tensor", "sv", vv)

        barrier()

        S_s1, S_mean, S_s2, S_var, S_msq, S_std, S_rstd = range(7)
        S_m1, S_m2 = 8, 9

        def stv(par, k):
            return stats[:, par * 16 + k:par * 16 + k + 1]

        consumed = {}
        slot_ev = [None] * 4
        gate_ps_free = None
        prev_gmm = None
        xn_done = {}
        tr_mark = {}

        def emit_ln_a(xt):
            par = xt % 2
            sxn = "sx0" if par == 0 else "sx1"
            if xt >= 2:
                sm, vm = consumed[xt - 2]
                W("sync", "ss", sm)
                W("sync", "sv", vm)
            vx = DMA("sync", sxn, xstage[par][:, :], x_d[xt * 128:(xt + 1) * 128, :])
            W("scalar", sxn, vx)
            if xt - 2 in tr_mark:
                W("scalar", "smm", tr_mark[xt - 2])
            OP("scalar", "ss", lambda E, p=par: E.activation(
                xnf[p][:, :], xstage[p][:, :], AF.Identity, accum_out=stv(p, S_s1)))
            W("scalar", "ss", r.val("ss"))
            sq_ = OP("scalar", "ss", lambda E, p=par: E.activation(
                xnf[p][:, :], xstage[p][:, :], AF.Square, accum_out=stv(p, S_s2)))
            W("vector", "ss", sq_)
            VOP(lambda E, p=par: E.tensor_scalar(stv(p, S_mean), stv(p, S_s1),
                                                 1.0 / HID, None, ALU.mult))
            VOP(lambda E, p=par: E.tensor_scalar(stv(p, S_var), stv(p, S_s2),
                                                 1.0 / HID, 1e-5, ALU.mult, ALU.add))
            VCH()
            VOP(lambda E, p=par: E.tensor_tensor(stv(p, S_msq), stv(p, S_mean),
                                                 stv(p, S_mean), ALU.mult))
            VCH()
            vv = VOP(lambda E, p=par: E.tensor_tensor(stv(p, S_var), stv(p, S_var),
                                                      stv(p, S_msq), ALU.subtract))
            W("scalar", "sv", vv)
            OP("scalar", "ss", lambda E, p=par: E.activation(
                stv(p, S_std), stv(p, S_var), AF.Ln))
            W("scalar", "ss", r.val("ss"))
            sv_ = OP("scalar", "ss", lambda E, p=par: E.activation(
                stv(p, S_rstd), stv(p, S_std), AF.Exp, scale=-0.5))
            consumed[xt] = (sv_, None)
            return sv_

        def emit_ln_b(xt, sv_):
            par = xt % 2
            W("vector", "ss", sv_)
            vv = VOP(lambda E, p=par: E.tensor_scalar(
                xnf[p][:, :], xstage[p][:, :], stv(p, S_mean), stv(p, S_rstd),
                ALU.subtract, ALU.mult))
            consumed[xt] = (sv_, vv)
            xn_done[xt] = vv

        def emit_tr(xt):
            nonlocal gate_ps_free, prev_gmm
            par = xt % 2
            own = xt < NOT_
            W("tensor", "sv", xn_done[xt])
            if own and prev_gmm is not None:
                W("vector", "smm", prev_gmm)
            dstb = xnT if own else bigb
            dsto = xt * 128 if own else (xt - NOT_) * 128
            for t4 in range(4):
                WM("tensor", slot_ev[0])
                for k4 in range(4):
                    t = t4 * 4 + k4
                    tv = OP("tensor", "smm", lambda E, tt=t, s=k4, p=par: E.transpose(
                        PS[:, s * 512:s * 512 + 128], xnf[p][:, tt * 128:(tt + 1) * 128],
                        idf[:, :]))
                W("vector", "smm", tv)
                ps4 = AP(PS[:, :].tensor, PS[:, :].offset, [[4096, 128], [512, 4], [1, 128]])
                dst4 = AP(dstb[:, :].tensor, dstb[:, :].offset + t4 * 4 * OWN + dsto,
                          [[NT * OWN, 128], [OWN, 4], [1, 128]])
                VOP(lambda E, a=dst4, b=ps4: E.tensor_copy(a, b))
                if own:
                    VCH()
                    lo4 = xnloT[:, t4 * 512:(t4 + 1) * 512]
                    VOP(lambda E, a=lo4, b=ps4, c=dst4: E.tensor_tensor(a, b, c, ALU.subtract))
                slot_ev[0] = ("sv", r.val("sv"))
            if own:
                vlo = r.val("sv")
                W("tensor", "sv", vlo)
                WM("tensor", gate_ps_free)
                n_mm = 0
                for t in range(NT):
                    hi = xnT[:, t * OWN + xt * 128: t * OWN + (xt + 1) * 128]
                    lo = xnloT[:, t * 128:(t + 1) * 128]
                    whi = wgT_hi[:, t * 8:(t + 1) * 8]
                    wlo = wgT_lo[:, t * 8:(t + 1) * 8]
                    for lh, rh in ((hi, whi), (hi, wlo), (lo, whi), (lo, wlo)):
                        n_mm += 1
                        MM(PS[:, 2048:2056], lh, rh, n_mm == 1, n_mm == 64,
                           inc=1 if n_mm == 64 else 0)
                prev_gmm = r.val("smm")
                W("vector", "smm", prev_gmm)
                VOP(lambda E: E.reduce_max(stv(0, S_m1), PS[:, 2048:2056], axis=AX.X))
                VCH()
                VOP(lambda E, q=xt: E.tensor_scalar(
                    eq_sb[:, q * 8:(q + 1) * 8], PS[:, 2048:2056], stv(0, S_m1),
                    None, ALU.is_equal))
                VCH()
                VOP(lambda E, q=xt: E.scalar_tensor_tensor(
                    tmp8[:, :], eq_sb[:, q * 8:(q + 1) * 8], -1e30, PS[:, 2048:2056],
                    ALU.mult, ALU.add))
                VCH()
                VOP(lambda E: E.reduce_max(stv(0, S_m2), tmp8[:, :], axis=AX.X))
                VCH()
                VOP(lambda E, q=xt: E.tensor_tensor(
                    dm_sb[:, q:q + 1], stv(0, S_m1), stv(0, S_m2), ALU.subtract))
                vv = VOP(lambda E, q=xt: E.tensor_scalar(
                    eq_sb[:, (NOT_ + q) * 8:(NOT_ + q + 1) * 8], tmp8[:, :],
                    stv(0, S_m2), None, ALU.is_equal))
                gate_ps_free = ("sv", vv)
            tr_mark[xt] = r.val("smm")

        ln_marks = {}
        for xt in range(NXT):
            ln_marks[xt] = emit_ln_a(xt)
            if xt > 0:
                emit_tr(xt - 1)
            emit_ln_b(xt, ln_marks[xt])
        emit_tr(NXT - 1)
        W("scalar", "sv", r.val("sv"))
        sv_ = OP("scalar", "ss", lambda E: E.activation(
            w12_sb[:, 0:NOT_], dm_sb[:, :], AF.Sigmoid))
        W("vector", "ss", sv_)
        VOP(lambda E: E.tensor_scalar(w12_sb[:, NOT_:2 * NOT_], w12_sb[:, 0:NOT_],
                                      -1.0, 1.0, ALU.mult, ALU.add))
        VCH()
        for q in range(NOT_):
            VOP(lambda E, qq=q: E.tensor_scalar(
                tmp8[:, :], eq_sb[:, qq * 8:(qq + 1) * 8],
                w12_sb[:, qq:qq + 1], None, ALU.mult))
            VCH()
            VOP(lambda E, qq=q: E.scalar_tensor_tensor(
                C_sb[:, qq * 8:(qq + 1) * 8],
                eq_sb[:, (NOT_ + qq) * 8:(NOT_ + qq + 1) * 8],
                w12_sb[:, NOT_ + qq:NOT_ + qq + 1], tmp8[:, :], ALU.mult, ALU.add))
            VCH()
        vC = r.val("sv")
        if debug:
            W("sync", "sv", vC)
            DMA("sync", "s0", csbg_d, C_sb[:, :])

        barrier()
        VOP(lambda E: E.memset(KTz[0][64:128, :], 0.0))
        VOP(lambda E: E.memset(KTz[1][0:64, :], 0.0))
        VOP(lambda E: E.memset(AP(ar_h, 16384 + 64, [[AR, 128], [130, NKT]]), 1.0))
        v_b0 = VOP(lambda E: E.memset(AP(ar_h, 16384 + 129, [[AR, 128], [130, NKT]]), 1.0))
        W("tensor", "sv", v_b0)

        W("sync", "sgsh", gsh_full)

        def wslice(wd, p):
            return [AP(wd.tensor, t0 * 128 * HID + p * 128,
                       [[HID, 128], [128 * HID, 4], [1, 128]]) for t0 in (0, 4, 8, 12)]

        def load_pair_w(p, wait_mark):
            for wd, dst in ((wq_d, wq_sb), (wk_d, wk_sb), (wv_d, wv_sb)):
                for i, src in enumerate(wslice(wd, p)):
                    WM("gpsimd", wait_mark)
                    wait_mark = None
                    DMA("gpsimd", "sw", dst[:, i * 512:(i + 1) * 512], src)
            return r.val("sw")

        w_ready = load_pair_w(0, ("sv", vC))
        exp_by_slot = [0, 0]
        av_by_abuf = {}
        psA_free = [None, None]
        psS_free = [None, None]
        vproj_end = {}
        sao_last = 0
        srb_last = 0
        srh_last = 0
        ao_mult_last = None
        bias_slot_free = [None, None, None, None]

        for p in range(NPAIR):
            W("tensor", "sw", w_ready)
            W("tensor", "ss", max(exp_by_slot))
            for t in range(NT):
                for c4 in range(4):
                    MM(PS[:, c4 * 512:(c4 + 1) * 512], wq_sb[:, t * 128:(t + 1) * 128],
                       xnT[:, t * OWN + c4 * 512: t * OWN + (c4 + 1) * 512],
                       t == 0, t == NT - 1,
                       inc=1 if (t == NT - 1 and c4 == 3) else 0)
            qmm = r.val("smm")
            W("vector", "smm", qmm)
            VOP(lambda E: E.tensor_copy(QT2[:, :], PS[:, 0:2048]))
            kv_ev = [None, None]
            WM("tensor", psA_free[0])
            WM("tensor", psA_free[1])
            for kc in range(8):
                bk = 2048 + (kc % 2) * 512
                src_blk = xnT if kc < 4 else bigb
                WM("tensor", kv_ev[kc % 2])
                for t in range(NT):
                    MM(PS[:, bk:bk + 512], wk_sb[:, t * 128:(t + 1) * 128],
                       src_blk[:, t * OWN + (kc % 4) * 512: t * OWN + (kc % 4 + 1) * 512],
                       t == 0, t == NT - 1, inc=1 if t == NT - 1 else 0)
                kmm = r.val("smm")
                W("vector", "smm", kmm)
                VOP(lambda E, b=bk, c=kc: E.tensor_copy(
                    KTz[0][0:64, c * 512:(c + 1) * 512], PS[0:64, b:b + 512]))
                VOP(lambda E, b=bk, c=kc: E.tensor_copy(
                    KTz[1][64:128, c * 512:(c + 1) * 512], PS[64:128, b:b + 512]))
                kv_ev[kc % 2] = ("sv", r.val("sv"))
            tr_ev = [psA_free[1], None]
            psA_free[1] = None
            for kc in range(8):
                bk = 2048 + (kc % 2) * 512
                src_blk = xnT if kc < 4 else bigb
                WM("tensor", kv_ev[kc % 2])
                for t in range(NT):
                    MM(PS[:, bk:bk + 512], wv_sb[:, t * 128:(t + 1) * 128],
                       src_blk[:, t * OWN + (kc % 4) * 512: t * OWN + (kc % 4 + 1) * 512],
                       t == 0, t == NT - 1, inc=1 if t == NT - 1 else 0)
                vmm = r.val("smm")
                W("vector", "smm", vmm)
                VOP(lambda E, b=bk, c=kc: E.tensor_copy(vtf[c % 2][:, :], PS[:, b:b + 512]))
                kv_ev[kc % 2] = ("sv", r.val("sv"))
                W("tensor", "sv", kv_ev[kc % 2][1])
                for k4 in range(4):
                    kt = kc * 4 + k4
                    to = 3072 + (k4 % 2) * 512
                    WM("tensor", tr_ev[k4 % 2])
                    tv = OP("tensor", "smm", lambda E, c=kc, kk=k4, o=to: E.transpose(
                        PS[:, o:o + 128], vtf[c % 2][:, kk * 128:(kk + 1) * 128],
                        idf[:, :]))
                    W("vector", "smm", tv)
                    VOP(lambda E, c=kt, o=to: E.tensor_copy(
                        V2[:, c * 130:c * 130 + 64], PS[:, o:o + 64]))
                    VOP(lambda E, c=kt, o=to: E.tensor_copy(
                        V2[:, c * 130 + 65:c * 130 + 129], PS[:, o + 64:o + 128]))
                    tr_ev[k4 % 2] = ("sv", r.val("sv"))
            vproj_end[p] = r.val("smm")
            if p + 1 < NPAIR:
                w_ready = load_pair_w(p + 1, ("smm", vproj_end[p]))
            else:
                for t in range(NT):
                    WM("gpsimd", ("smm", vproj_end[p]) if t == 0 else None)
                    DMA("gpsimd", "swo", bigb[:, t * 2048:(t + 1) * 2048],
                        AP(wo_d.tensor, t * 128 * HID, [[HID, 128], [1, HID]]))
            W("tensor", "sv", r.val("sv"))

            for h in range(2):
                hg = 2 * p + h
                for qc in range(NQC):
                    aslot = qc % 2
                    psA0 = 2048 + aslot * 1024

                    def emit_S(kt):
                        cls = _classify(kt, qc)
                        sslot = kt % 2
                        psS = PS[:, sslot * 1024:(sslot + 1) * 1024]
                        W("tensor", "ss", exp_by_slot[sslot])
                        for j in range(2):
                            MM(PS[:, sslot * 1024 + j * 512:sslot * 1024 + (j + 1) * 512],
                               KTz[h][:, kt * 128:(kt + 1) * 128],
                               QT2[:, qc * 1024 + j * 512:qc * 1024 + (j + 1) * 512],
                               True, True, inc=1 if j == 1 else 0)
                        smm_v = r.val("smm")
                        bias_m = None
                        if cls[0] == "band":
                            row = cls[1] + hg
                            i0 = cls[2]
                            bslot = kt % 4
                            sbn = "sbb%d" % bslot
                            WM("sync", bias_slot_free[bslot])
                            bias_slot_free[bslot] = None
                            vb = DMA("sync", sbn, btile[bslot][:, :],
                                     AP(gsh_d.tensor, row * 128 * GW + i0,
                                        [[GW, 128], [1, QCW]]))
                            W("vector", sbn, vb)
                            W("vector", "smm", smm_v)
                            bias_m = VOP(lambda E, ps=psS, bt=btile[bslot]:
                                         E.tensor_tensor(ps, ps, bt, ALU.add))
                            bcol = m60[:, 0:1]
                        elif cls[0] == "c0":
                            bcol = ctab[:, hg:hg + 1]
                        elif cls[0] == "c31":
                            bcol = ctab[:, 32 + hg:32 + hg + 1]
                        else:
                            bcol = ctab[:, 64 + hg:64 + hg + 1]
                        if bias_m is not None:
                            W("scalar", "sv", bias_m)
                        else:
                            W("scalar", "smm", smm_v)
                        am = av_by_abuf.get(kt % 2)
                        if am is not None:
                            W("scalar", "smm", am)
                        ev = OP("scalar", "ss", lambda E, ps=psS,
                                ab=abuf[0][kt % 2], bc=bcol:
                                E.activation(ab, ps, AF.Exp, bias=bc))
                        exp_by_slot[sslot] = ev
                        if bias_m is not None:
                            bias_slot_free[bslot] = ("ss", ev)

                    def emit_AV(kt):
                        W("tensor", "ss", exp_by_slot[kt % 2])
                        if kt == 0:
                            WM("tensor", psA_free[aslot])
                            psA_free[aslot] = None
                        for j in range(2):
                            MM(PS[0:65, psA0 + j * 512:psA0 + (j + 1) * 512],
                               V2[:, kt * 130 + h * 65:kt * 130 + h * 65 + 65],
                               abuf[0][kt % 2][:, j * 512:(j + 1) * 512],
                               kt == 0, kt == NKT - 1, inc=1 if j == 1 else 0)
                        av_by_abuf[kt % 2] = r.val("smm")

                    for kt in range(NKT):
                        emit_S(kt)
                        if kt > 0:
                            emit_AV(kt - 1)
                    emit_AV(NKT - 1)
                    avl = av_by_abuf[(NKT - 1) % 2]
                    W("vector", "smm", avl)
                    WM("vector", ao_mult_last)
                    vr = VOP(lambda E, o=psA0: E.reciprocal(
                        rb_sb[64:65, :], PS[64:65, o:o + 1024]))
                    W("sync", "sv", vr)
                    W("sync", "srb", srb_last)
                    rsl = (2 * hg + qc) % 2
                    srh_last = DMA("sync", "srh", rch_d[rsl:rsl + 1, :], rb_sb[64:65, :])
                    W("sync", "srh", srh_last)
                    srb_last = DMA("sync", "srb", rb_sb[0:64, :],
                                   AP(rch_d.tensor, rsl * QCW, [[0, 64], [1, QCW]]))
                    W("vector", "srb", srb_last)
                    if sao_last:
                        W("vector", "sao", sao_last)
                    vao = VOP(lambda E, o=psA0: E.tensor_tensor(
                        aostage[0:64, :], PS[0:64, o:o + 1024],
                        rb_sb[0:64, :], ALU.mult))
                    ao_mult_last = ("sv", vao)
                    psA_free[aslot] = ("sv", vao)
                    W("sync", "sv", vao)
                    sao_last = DMA("sync", "sao",
                                   aoT_d[hg * 64:(hg + 1) * 64, qc * 1024:(qc + 1) * 1024],
                                   aostage[0:64, :])

        barrier()

        r.cuts.append({e: len(r.q[e]) for e in ENGS})
        if not os.environ.get("K_PC_NOSAO"):
            W("sync", "sao", r.val("sao"))
        if not os.environ.get("K_PC_NOWO"):
            W("tensor", "swo", r.val("swo"))
        o_ev = [None, None]
        o_mm_mark = [0, 0]
        sat_hist = {}
        for tt in range(NOT_):
            par = tt % 2
            sem = "sac0" if par == 0 else "sac1"
            if tt >= 2:
                W("sync", "smm", o_mm_mark[par])
            vl = 0
            if not os.environ.get("K_PC_NOLD"):
                for t0 in (0, 4, 8, 12):
                    vl = DMA("sync", sem, aoc[par][:, t0 * 128:(t0 + 4) * 128],
                             AP(aoT_d.tensor, t0 * 128 * OWN + tt * 128,
                                [[OWN, 128], [128 * OWN, 4], [1, 128]]))
                W("tensor", sem, vl)
            WM("tensor", o_ev[par])
            if os.environ.get("K_PC_NOMM"):
                o_mm_mark[par] = OP("tensor", "smm", lambda E: E.nop())
            else:
                for t in range(NT):
                    for hc in range(4):
                        MM(PS[:, par * 2048 + hc * 512:par * 2048 + (hc + 1) * 512],
                           aoc[par][:, t * 128:(t + 1) * 128],
                           bigb[:, t * 2048 + hc * 512:t * 2048 + (hc + 1) * 512],
                           t == 0, t == NT - 1,
                           inc=1 if (t == NT - 1 and hc == 3) else 0)
                o_mm_mark[par] = r.val("smm")
            W("vector", "smm", o_mm_mark[par])
            satn = "sat0" if par == 0 else "sat1"
            if tt >= 2 and sat_hist.get(tt - 2):
                W("vector", satn, sat_hist[tt - 2])
            if os.environ.get("K_PC_NOST"):
                vv = VOP(lambda E: E.nop())
                o_ev[par] = ("sv", vv)
                sat_hist[tt] = 0
            else:
                vv = VOP(lambda E, pp=par: E.tensor_copy(ostg[pp][:, :],
                                                         PS[:, pp * 2048:(pp + 1) * 2048]))
                o_ev[par] = ("sv", vv)
                W("sync", "sv", vv)
                sat_hist[tt] = DMA("sync", satn, at32_d[tt * 128:(tt + 1) * 128, :],
                                   ostg[par][:, :])

        barrier()

        omm_last = r.val("smm")
        m_gTm = {}
        m_sttbank = [None] * 4
        m_f1 = {}
        stt_prev = {}
        sl_free = None
        for half in range(2):
            W("gpsimd", "smm", omm_last)
            if half == 1:
                W("gpsimd", "sout", 16)
            for j0 in (0, 1):
                gi0 = half * NE * NFT + j0
                e0 = 0
                wm0 = m_f1.get(gi0 - 2)
                for ab, foff in ((0, 0), (2, FF)):
                    WM("gpsimd", ("smm", wm0) if wm0 else None)
                    wm0 = None
                    DMA("gpsimd", "sw1a%d" % (gi0 % 2) if ab == 0 else "sw1b%d" % (gi0 % 2),
                        w1s[ab + gi0 % 2][:, :],
                        AP(w1_d.tensor, e0 * HID * FF2 + foff + j0 * 128,
                           [[FF2, 128], [128 * FF2, NT], [1, 128]]))
            W("gpsimd", "sat0", r.val("sat0"))
            W("gpsimd", "sat1", r.val("sat1"))
            for tt in range(8):
                DMA("gpsimd", "sgx", outacc[:, tt * 2048:(tt + 1) * 2048],
                    x_d[half * 1024 + tt * 128: half * 1024 + (tt + 1) * 128, :])
            W("gpsimd", "sgx", r.val("sgx"))
            for tt in range(8):
                DMA("gpsimd", "sga", outacc[:, tt * 2048:(tt + 1) * 2048],
                    at32_d[half * 1024 + tt * 128: half * 1024 + (tt + 1) * 128, :],
                    accum_op=ALU.add)
            sga_full = r.val("sga")
            for e in range(NE):
                for j in range(NFT):
                    gi = half * NE * NFT + e * NFT + j
                    slot = gi % 2
                    if not (e == 0 and j < 2):
                        wm = m_f1.get(gi - 2)
                        for ab, foff, sname in ((0, 0, "sw1a%d" % slot),
                                                (2, FF, "sw1b%d" % slot)):
                            WM("gpsimd", ("smm", wm) if wm else None)
                            wm = None
                            DMA("gpsimd", sname, w1s[ab + slot][:, :],
                                AP(w1_d.tensor, e * HID * FF2 + foff + j * 128,
                                   [[FF2, 128], [128 * FF2, NT], [1, 128]]))
                    a_ps = 0 if slot == 0 else 2048
                    b_ps = 1024 if slot == 0 else 3072
                    W("tensor", "sw1a%d" % slot, r.val("sw1a%d" % slot))
                    W("tensor", "sw1b%d" % slot, r.val("sw1b%d" % slot))
                    WM("tensor", m_gTm.get(gi - 2))
                    if slot == 0:
                        for bq in range(4):
                            WM("tensor", m_sttbank[bq])
                            m_sttbank[bq] = None
                    for ab, ps0 in ((0, a_ps), (2, b_ps)):
                        for t in range(NT):
                            for c2 in range(2):
                                MM(PS[:, ps0 + c2 * 512:ps0 + (c2 + 1) * 512],
                                   w1s[ab + slot][:, t * 128:(t + 1) * 128],
                                   xnT[:, t * OWN + half * 1024 + c2 * 512:
                                       t * OWN + half * 1024 + (c2 + 1) * 512],
                                   t == 0, t == NT - 1,
                                   inc=1 if (t == NT - 1 and c2 == 1) else 0)
                    m_f1[gi] = r.val("smm")
                    W("scalar", "smm", m_f1[gi])
                    WM("scalar", sl_free)
                    sv_ = OP("scalar", "ss", lambda E, a=a_ps: E.activation(
                        sl_sb[:, :], PS[:, a:a + 1024], AF.Silu))
                    W("vector", "ss", sv_)
                    vv = VOP(lambda E, b=b_ps, jj=j: E.tensor_tensor(
                        gT[:, jj * QCW:(jj + 1) * QCW], sl_sb[:, :],
                        PS[:, b:b + 1024], ALU.mult))
                    m_gTm[gi] = ("sv", vv)
                    sl_free = ("sv", vv)
                gT_ready = m_gTm[half * NE * NFT + e * NFT + NFT - 1]
                for hc in range(4):
                    g2i = half * NE * 4 + e * 4 + hc
                    slot = g2i % 2
                    wfree = m_f1.get(gi) if g2i < 2 else None
                    w2war = stt_prev.get(("w2", slot))
                    WM("gpsimd", w2war)
                    DMA("gpsimd", "sw2_%d" % slot, w2c[slot][:, :],
                        AP(w2_d.tensor, e * FF * HID + hc * 512,
                           [[HID, 128], [128 * HID, NFT], [1, 512]]))
                    w2v = r.val("sw2_%d" % slot)
                    W("tensor", "sw2_%d" % slot, w2v)
                    WM("tensor", gT_ready)
                    gT_ready = None
                    for tt in range(8):
                        bk = tt % 4
                        WM("tensor", m_sttbank[bk])
                        m_sttbank[bk] = None
                        for ft in range(NFT):
                            MM(PS[:, bk * 512:(bk + 1) * 512],
                               gT[:, ft * QCW + tt * 128:ft * QCW + (tt + 1) * 128],
                               w2c[slot][:, ft * 512:(ft + 1) * 512],
                               ft == 0, ft == NFT - 1,
                               inc=1 if ft == NFT - 1 else 0)
                        ymm = r.val("smm")
                        W("vector", "smm", ymm)
                        W("vector", "sga", sga_full)
                        pk = (half, tt, hc)
                        if pk in stt_prev:
                            W("vector", "sv", stt_prev[pk])
                        vv = VOP(lambda E, t_=tt, h_=hc, b=bk, ee=e, hf=half:
                                 E.scalar_tensor_tensor(
                                     outacc[:, t_ * 2048 + h_ * 512:
                                            t_ * 2048 + (h_ + 1) * 512],
                                     PS[:, b * 512:(b + 1) * 512],
                                     C_sb[:, (hf * 8 + t_) * 8 + ee:
                                          (hf * 8 + t_) * 8 + ee + 1],
                                     outacc[:, t_ * 2048 + h_ * 512:
                                            t_ * 2048 + (h_ + 1) * 512],
                                     ALU.mult, ALU.add))
                        stt_prev[pk] = vv
                        m_sttbank[bk] = ("sv", vv)
                    stt_prev[("w2", slot)] = ("smm", r.val("smm"))
            W("sync", "sv", r.val("sv"))
            DMA("sync", "sout",
                AP(out_d.tensor, half * 1024 * HID,
                   [[HID, 128], [128 * HID, 8], [1, HID]]),
                outacc[:, :].rearrange("p (t c) -> p t c", t=8))
        W("sync", "sout", r.val("sout"))

        for e in ENGS:
            n = len(r.q[e]) if cut is None else r.cuts[cut][e]
            getattr(blk, e)(lambda E, Q=tuple(r.q[e][:n]): [f(E) for f in Q])

    build.last_rec = r
    return nc


_NC_CACHE = {}


def _get_nc(debug=False):
    if debug not in _NC_CACHE:
        _NC_CACHE[debug] = build(debug)
    return _NC_CACHE[debug]


def _onehot_tables(half):
    i = np.arange(GTAB, dtype=np.int64)
    d_own = i - 4095
    d_par = d_own if half == 0 else d_own + 4096
    oh_o = np.zeros((NB, GTAB), dtype=ml_dtypes.bfloat16)
    oh_p = np.zeros((NB, GTAB), dtype=ml_dtypes.bfloat16)
    oh_o[_bucket_np(d_own), i] = 1
    oh_p[_bucket_np(d_par), i] = 1
    return oh_o, oh_p


def _make_inputs(inputs):
    x = np.asarray(inputs["L0_attn_hidden_states"])
    common = {
        "wq": np.ascontiguousarray(inputs["L0_attn_Wq"], dtype=np.float32),
        "wk": np.ascontiguousarray(inputs["L0_attn_Wk"], dtype=np.float32),
        "wv": np.ascontiguousarray(inputs["L0_attn_Wv"], dtype=np.float32),
        "wo": np.ascontiguousarray(inputs["L0_attn_Wo"], dtype=np.float32),
        "relb": np.ascontiguousarray(inputs["L0_attn_rel_bias"], dtype=np.float32),
        "wg": np.ascontiguousarray(inputs["L0_ffn_wg"], dtype=np.float32),
        "w1": np.ascontiguousarray(inputs["L0_ffn_w1"], dtype=np.float32),
        "w2": np.ascontiguousarray(inputs["L0_ffn_w2"], dtype=np.float32),
        "identf": np.eye(128, dtype=np.float32),
    }
    ohs = [_onehot_tables(h) for h in (0, 1)]
    in_maps = []
    for c in range(8):
        b, half = c // 2, c % 2
        xr = np.ascontiguousarray(np.concatenate(
            [x[b, half * OWN:(half + 1) * OWN],
             x[b, (1 - half) * OWN:(2 - half) * OWN]], axis=0).astype(np.float32))
        m = dict(common)
        m["x"] = xr
        m["oho"], m["ohp"] = ohs[half]
        in_maps.append(m)
    return in_maps


def kernel(**inputs):
    nc = _get_nc(False)
    in_maps = _make_inputs(inputs)
    res = run_bass_kernel_spmd(nc, in_maps, core_ids=list(range(8)))
    out = np.empty((4, SEQ, HID), dtype=np.float32)
    for c in range(8):
        b, half = c // 2, c % 2
        out[b, half * OWN:(half + 1) * OWN] = res.results[c]["out"]
    return out
